# version 01 (final)
# baseline (speedup 1.0000x reference)
import numpy as np
from contextlib import ExitStack
import concourse.bass as bass
import concourse.mybir as mybir
from concourse.bass_utils import run_bass_kernel_spmd
import ml_dtypes

F32 = mybir.dt.float32
BF16 = mybir.dt.bfloat16
I32 = mybir.dt.int32
AF = mybir.ActivationFunctionType
ALU = mybir.AluOpType
AX = mybir.AxisListType

B, T, D = 4, 4096, 1024
NF = D // 128
TL = 2048
NT = TL // 512
EPS = 1e-5
bf16 = ml_dtypes.bfloat16

KB = 1024
SBUF_LIMIT = 207 * KB
OFF_CONST = 0
OFF_MISC = 2048
A_WB0 = 11520
A_WB1 = A_WB0 + 16 * KB
OFF_XLN = A_WB1 + 16 * KB
OFF_X = OFF_XLN + 32 * KB
OFF_LNS = OFF_X + 64 * KB
LNS_SQ = OFF_LNS
LNS_ROWS = OFF_LNS + 8 * KB
LNS_AB = LNS_ROWS + 18 * KB
OFF_TAIL3 = LNS_AB + 8 * KB
A_QFM = OFF_LNS
A_KH = OFF_TAIL3
A_VH = OFF_TAIL3 + 16 * KB
A_PT = OFF_TAIL3 + 16 * KB + 8320
A_FSB = A_PT + 4 * KB
A_XTR = A_FSB + KB
A_SRBF = OFF_MISC + 7424
P_KST = OFF_TAIL3
P_VSTG = OFF_TAIL3 + 8 * KB
P_AX = OFF_TAIL3 + 12 * KB
A_MISC = OFF_MISC
A_VSTG = P_VSTG



def _np(a):
    return np.ascontiguousarray(np.asarray(a), dtype=np.float32)


def _tr(a):
    return np.ascontiguousarray(np.asarray(a, dtype=np.float32).T)


def _col128(v):
    v = np.asarray(v, dtype=np.float32)
    n = v.shape[0] // 128
    return np.ascontiguousarray(v.reshape(n, 128).T)


def _row(v):
    v = np.asarray(v, dtype=np.float32)
    return np.ascontiguousarray(v.reshape(1, -1))


def core_token_slices(core):
    p = core % 2
    return [((2 * b + p) * 512, (2 * b + p) * 512 + 512) for b in range(4)]


def local_to_global(core, j):
    return (2 * (j // 512) + core % 2) * 512 + j % 512


def shard_x(x):
    out = []
    for c in range(8):
        b = c // 2
        parts = [x[b, s:e, :] for (s, e) in core_token_slices(c)]
        xt = np.concatenate(parts, axis=0)
        out.append(np.ascontiguousarray(xt.T))
    return out


def unshard_x(shards):
    x = np.zeros((B, T, D), dtype=np.float32)
    for c in range(8):
        b = c // 2
        xt = np.asarray(shards[c]).T
        o = 0
        for (s, e) in core_token_slices(c):
            x[b, s:e, :] = xt[o:o + (e - s)]
            o += e - s
    return x


def host_consts(core):
    return {
        "ones_col": np.ones((128, 1), dtype=np.float32),
        "ones_row128": np.ones((1, 128), dtype=np.float32),
        "ones_row_t": np.ones((1, 512), dtype=np.float32),
        "cpool": np.tile(np.array([[EPS, 1.0, 0.0, 0.5]], dtype=np.float32), (128, 1)),
        "ident": np.eye(128, dtype=np.float32),
    }


CONST_SPECS = [("ones_col", [128, 1], BF16), ("ones_row128", [1, 128], BF16),
               ("ones_row_t", [1, 512], BF16), ("cpool", [128, 4], F32),
               ("ident", [128, 128], BF16)]



def _pbytes(shape, dtype):
    n = 1
    for s in shape[1:]:
        n *= s
    return n * {F32: 4, BF16: 2, I32: 4}[dtype]


class Ctx:
    def __init__(self, nc, block, stack):
        self.nc = nc
        self.block = block
        self.stack = stack
        self._sems = {}
        self.lo = (nc.sbuf_base + 63) & ~63
        assert nc.sbuf_top - self.lo >= SBUF_LIMIT, (nc.sbuf_base, nc.sbuf_top)
        self.psA = stack.enter_context(nc.psum_tensor("psA", [128, 2048], F32))
        self.psB = stack.enter_context(nc.psum_tensor("psB", [128, 2048], F32))
        self.ps = [self.psA[:, i * 512:(i + 1) * 512] for i in range(4)] + \
                  [self.psB[:, i * 512:(i + 1) * 512] for i in range(4)]
        self.bank_free = [None] * 8
        self.bank_ready = [None] * 8
        self._uid = 0

    def at(self, name, shape, dtype, off):
        assert off + _pbytes(shape, dtype) <= SBUF_LIMIT, \
            (name, off, _pbytes(shape, dtype))
        return self.nc.alloc_sbuf_tensor_at(name + str(self.uid()), list(shape),
                                            dtype, offset=self.lo + off)

    def uid(self):
        self._uid += 1
        return self._uid

    def sem(self, name):
        if name not in self._sems:
            h = self.stack.enter_context(self.nc.semaphore(name))
            self._sems[name] = [h, 0]
        return self._sems[name]

    def inc(self, name, instr, by=1):
        s = self.sem(name)
        s[1] += by
        instr.then_inc(s[0], by)
        return s[1]

    def val(self, name):
        return self.sem(name)[1]

    def wait(self, eng, name, target=None):
        s = self.sem(name)
        t = s[1] if target is None else target
        if t > 0:
            eng.wait_ge(s[0], t)

    def wait_bank_free(self, eng, i):
        if self.bank_free[i] is not None:
            ch, cnt = self.bank_free[i]
            self.wait(eng, ch, cnt)
        self.bank_free[i] = None

    def set_bank_ready(self, i, ch, cnt):
        self.bank_ready[i] = (ch, cnt)

    def wait_bank_ready(self, eng, i):
        ch, cnt = self.bank_ready[i]
        self.wait(eng, ch, cnt)

    def release_bank(self, i, ch, cnt):
        self.bank_free[i] = (ch, cnt)



def emit_load_x(ctx, x_sb, x_dram, tl=TL):
    blk = ctx.block

    @blk.sync
    def _(sy):
        for c in range(NF):
            ins = sy.dma_start(out=x_sb[:, c * tl:(c + 1) * tl],
                               in_=x_dram[c * 128:(c + 1) * 128, :])
            ctx.inc("ld", ins, 16)


def emit_store_x(ctx, x_sb, y_dram, tl=TL):
    blk = ctx.block

    @blk.sync
    def _(sy):
        ctx.wait(sy, "ve")
        ctx.wait(sy, "se")
        for c in range(NF):
            ins = sy.dma_start(out=y_dram[c * 128:(c + 1) * 128, :],
                               in_=x_sb[:, c * tl:(c + 1) * tl])
            ctx.inc("st", ins, 16)
        ctx.wait(sy, "st")


def emit_ln(ctx, st, tl=TL, nf=NF, src_wait=None, src=None, dst=None,
            affine=None, inplace=False, stats_only=False):
    blk = ctx.block
    x_sb = src if src is not None else st["x"]
    xln_sb = dst if dst is not None else st["xln"]
    sq_sb, rows_sb = st["sq"], st["rows"]
    consts = st["consts"]
    abrow = st["abrow"]
    nt = tl // 512
    dinv = 1.0 / (nf * 128)
    SU, SQ, SC = 0, tl, 2 * tl

    waits = []
    if src_wait:
        waits = src_wait if isinstance(src_wait, list) else [src_wait]

    if not inplace:
        @blk.vector
        def _(v):
            for w in waits:
                ctx.wait(v, *w)
            ins = v.tensor_copy(xln_sb[:, 0:nf * tl], x_sb[:, 0:nf * tl])
            ctx.inc("ve", ins)

    ve_copy = ctx.val("ve")

    sq_slots = [None, None]
    for c in range(nf):
        half = c % 2

        @blk.scalar
        def _(s, c=c, half=half):
            for w in waits:
                ctx.wait(s, *w)
            if sq_slots[half] is not None:
                ctx.wait(s, *sq_slots[half])
            ins = s.activation(sq_sb[:, half * tl:(half + 1) * tl],
                               x_sb[:, c * tl:(c + 1) * tl], AF.Square)
            ctx.inc("se", ins)
        se_now = ctx.val("se")

        @blk.tensor
        def _(t, c=c, half=half, se_now=se_now):
            if c == 0:
                ctx.wait(t, "ve", ve_copy)
                for bk in range(8):
                    ctx.wait_bank_free(t, bk)
            ctx.wait(t, "se", se_now)
            ins = None
            for tt in range(nt):
                sl = slice(c * tl + tt * 512, c * tl + (tt + 1) * 512)
                t.matmul(ctx.ps[tt][0:1, :], consts["ones_col"][:, :], xln_sb[:, sl],
                         start=(c == 0), stop=(c == nf - 1))
                sl2 = slice(half * tl + tt * 512, half * tl + (tt + 1) * 512)
                ins = t.matmul(ctx.ps[4 + tt][0:1, :], consts["ones_col"][:, :],
                               sq_sb[:, sl2], start=(c == 0), stop=(c == nf - 1))
            cnt = ctx.inc("pe", ins)
            sq_slots[half] = ("pe", cnt)
            if c == nf - 1:
                for tt in range(nt):
                    ctx.set_bank_ready(tt, "pe", cnt)
                    ctx.set_bank_ready(4 + tt, "pe", cnt)

    @blk.vector
    def _(v):
        ctx.wait_bank_ready(v, 0)
        ins = None
        for tt in range(nt):
            v.tensor_copy(rows_sb[0:1, SU + tt * 512:SU + (tt + 1) * 512],
                          ctx.ps[tt][0:1, :])
            ins = v.tensor_copy(rows_sb[0:1, SQ + tt * 512:SQ + (tt + 1) * 512],
                                ctx.ps[4 + tt][0:1, :])
        cnt = ctx.inc("ve", ins)
        for bk in range(8):
            ctx.release_bank(bk, "ve", cnt)

    @blk.vector
    def _(v):
        ctx.wait(v, "ve")
        v.tensor_scalar_mul(rows_sb[0:1, SU:SU + tl], rows_sb[0:1, SU:SU + tl], dinv)
        v.tensor_scalar_mul(rows_sb[0:1, SQ:SQ + tl], rows_sb[0:1, SQ:SQ + tl], dinv)
        ins = None
        for i in range(nt):
            sl = slice(SU + i * 512, SU + (i + 1) * 512)
            sq_sl = slice(SQ + i * 512, SQ + (i + 1) * 512)
            v.tensor_tensor(rows_sb[0:1, SC:SC + 512], rows_sb[0:1, sl],
                            rows_sb[0:1, sl], ALU.mult)
            ins = v.tensor_tensor(rows_sb[0:1, sq_sl], rows_sb[0:1, sq_sl],
                                  rows_sb[0:1, SC:SC + 512], ALU.subtract)
        ctx.inc("ve", ins)

    @blk.scalar
    def _(s):
        ctx.wait(s, "ve")
        s.activation(rows_sb[0:1, SQ:SQ + tl], rows_sb[0:1, SQ:SQ + tl], AF.Ln,
                     bias=consts["cpool"][0:1, 0:1])
        ins = s.activation(rows_sb[0:1, SQ:SQ + tl], rows_sb[0:1, SQ:SQ + tl],
                           AF.Exp, scale=-0.5)
        ctx.inc("se", ins)

    @blk.vector
    def _(v):
        ctx.wait(v, "se")
        v.tensor_copy(abrow[0:1, 0:tl], rows_sb[0:1, SQ:SQ + tl])
        ins = v.scalar_tensor_tensor(abrow[0:1, tl:2 * tl], rows_sb[0:1, SU:SU + tl],
                                     -1.0, rows_sb[0:1, SQ:SQ + tl], ALU.mult, ALU.mult)
        ctx.inc("ve", ins)

    ve_rows = ctx.val("ve")
    if stats_only:
        return ve_rows

    for tt in range(nt):
        bA = 4 + (2 * tt) % 4
        bB = 4 + (2 * tt + 1) % 4

        @blk.tensor
        def _(t, tt=tt, bA=bA, bB=bB):
            ctx.wait(t, "ve", ve_rows)
            ctx.wait_bank_free(t, bA)
            ctx.wait_bank_free(t, bB)
            t.matmul(ctx.ps[bA][:, :], consts["ones_row128"][:, :],
                     abrow[0:1, tt * 512:(tt + 1) * 512], start=True, stop=True)
            ins = t.matmul(ctx.ps[bB][:, :], consts["ones_row128"][:, :],
                           abrow[0:1, tl + tt * 512:tl + (tt + 1) * 512],
                           start=True, stop=True)
            cnt = ctx.inc("pe", ins)
            ctx.set_bank_ready(bA, "pe", cnt)
            ctx.set_bank_ready(bB, "pe", cnt)

        @blk.vector
        def _(v, tt=tt, bA=bA, bB=bB):
            ctx.wait_bank_ready(v, bA)
            ins = None
            for c in range(nf):
                sl = slice(c * tl + tt * 512, c * tl + (tt + 1) * 512)
                v.tensor_tensor(xln_sb[:, sl], x_sb[:, sl], ctx.ps[bA][:, :], ALU.mult)
                ins = v.tensor_tensor(xln_sb[:, sl], xln_sb[:, sl], ctx.ps[bB][:, :],
                                      ALU.add)
            cnt = ctx.inc("ve", ins)
            ctx.release_bank(bA, "ve", cnt)
            ctx.release_bank(bB, "ve", cnt)

        if affine is not None:
            ve_aff = ctx.val("ve")

            @blk.scalar
            def _(s, tt=tt, ve_aff=ve_aff):
                ctx.wait(s, "ve", ve_aff)
                ins = None
                for c in range(nf):
                    sl = slice(c * tl + tt * 512, c * tl + (tt + 1) * 512)
                    ins = s.activation(xln_sb[:, sl], xln_sb[:, sl], AF.Identity,
                                       scale=affine[0](c), bias=affine[1](c))
                ctx.inc("se", ins)


def emit_weight_load(ctx, w_sb, w_dram, fi_n, cols, row0=0, col0=0, ch="wld",
                     gate=None):
    blk = ctx.block

    @blk.gpsimd
    def _(g):
        if gate:
            ctx.wait(g, *gate)
        for c in range(fi_n):
            ins = g.dma_start(out=w_sb[:, c * cols:(c + 1) * cols],
                              in_=w_dram[row0 + c * 128:row0 + (c + 1) * 128,
                                         col0:col0 + cols])
            ctx.inc(ch, ins, 16)
    return ctx.val(ch)


def emit_row_load(ctx, dst_sb, src_dram, ch="wld", gate=None):
    blk = ctx.block

    @blk.gpsimd
    def _(g):
        if gate:
            ctx.wait(g, *gate)
        ins = g.dma_start(out=dst_sb, in_=src_dram)
        ctx.inc(ch, ins, 16)
    return ctx.val(ch)


def emit_proj(ctx, src_sb, src_col0, fi_n, w_sb, w_cols, fo_list, tl, out_cb,
              bias_row=None, consts=None, wld_target=0, src_ready=None,
              extra=None, src_tile=None, wld_ch="wld"):
    blk = ctx.block
    nt = tl // 512
    for oi, fo in enumerate(fo_list):
        banks = [(oi % 2) * nt + k for k in range(nt)]
        assert nt <= 4

        @blk.tensor
        def _(t, fo=fo, banks=banks, oi=oi):
            if wld_target:
                ctx.wait(t, wld_ch, wld_target)
            if src_ready is not None:
                srl = src_ready if isinstance(src_ready, list) else [src_ready]
                for w in srl:
                    ctx.wait(t, *w)
            for bk in banks:
                ctx.wait_bank_free(t, bk)
            started = [False] * nt
            if bias_row is not None:
                for tt in range(nt):
                    t.matmul(ctx.ps[banks[tt]][:, :],
                             bias_row[0:1, fo * 128:(fo + 1) * 128],
                             consts["ones_row_t"][0:1, 0:512],
                             start=True, stop=False)
                started = [True] * nt
            ins = None
            for fi in range(fi_n):
                wt = w_sb[:, fi * w_cols + fo * 128: fi * w_cols + (fo + 1) * 128]
                for tt in range(nt):
                    if src_tile is not None:
                        rhs = src_tile(fi, tt)
                    else:
                        rhs = src_sb[:, src_col0 + fi * tl + tt * 512:
                                     src_col0 + fi * tl + (tt + 1) * 512]
                    last = (fi == fi_n - 1) and (extra is None)
                    ins = t.matmul(ctx.ps[banks[tt]][:, :], wt, rhs,
                                   start=not started[tt], stop=last)
                    started[tt] = True
            if extra is not None:
                for ei, (lf, rf) in enumerate(extra):
                    for tt in range(nt):
                        ins = t.matmul(ctx.ps[banks[tt]][:, :], lf(fo), rf(tt),
                                       start=False, stop=(ei == len(extra) - 1))
            cnt = ctx.inc("pe", ins)
            for bk in banks:
                ctx.set_bank_ready(bk, "pe", cnt)

        out_cb(fo, banks)


def evac_simple(ctx, dst_sb, dst_col0, tl, eng="scalar", func=AF.Copy, scale=1.0,
                accum=False, scale_col=None):
    blk = ctx.block
    nt = tl // 512

    def cb(fo, banks):
        if eng == "scalar":
            @blk.scalar
            def _(s, fo=fo, banks=banks):
                ins = None
                sc = scale if scale_col is None else scale_col(fo)
                for tt in range(nt):
                    ctx.wait_bank_ready(s, banks[tt])
                    sl = slice(dst_col0 + fo * tl + tt * 512,
                               dst_col0 + fo * tl + (tt + 1) * 512)
                    ins = s.activation(dst_sb[:, sl], ctx.ps[banks[tt]][:, :], func,
                                       scale=sc)
                cnt = ctx.inc("se", ins)
                for tt in range(nt):
                    ctx.release_bank(banks[tt], "se", cnt)
        else:
            @blk.vector
            def _(v, fo=fo, banks=banks):
                ins = None
                for tt in range(nt):
                    ctx.wait_bank_ready(v, banks[tt])
                    sl = slice(dst_col0 + fo * tl + tt * 512,
                               dst_col0 + fo * tl + (tt + 1) * 512)
                    if scale_col is not None:
                        ins = v.tensor_scalar(dst_sb[:, sl], ctx.ps[banks[tt]][:, :],
                                              scale_col(fo), None, ALU.mult)
                    elif accum:
                        ins = v.tensor_tensor(dst_sb[:, sl], dst_sb[:, sl],
                                              ctx.ps[banks[tt]][:, :], ALU.add)
                    else:
                        ins = v.tensor_copy(dst_sb[:, sl], ctx.ps[banks[tt]][:, :])
                cnt = ctx.inc("ve", ins)
                for tt in range(nt):
                    ctx.release_bank(banks[tt], "ve", cnt)
    return cb


def evac_resid(ctx, x_sb, tl=TL):
    blk = ctx.block
    nt = tl // 512

    def cb(fo, banks):
        @blk.vector
        def _(v, fo=fo, banks=banks):
            ins = None
            for tt in range(nt):
                ctx.wait_bank_ready(v, banks[tt])
                sl = slice(fo * tl + tt * 512, fo * tl + (tt + 1) * 512)
                ins = v.tensor_tensor(x_sb[:, sl], x_sb[:, sl],
                                      ctx.ps[banks[tt]][:, :], ALU.add)
            cnt = ctx.inc("ve", ins)
            for tt in range(nt):
                ctx.release_bank(banks[tt], "ve", cnt)
    return cb


def emit_bigproj(ctx, src_tile, fi_n, w_dram, fo_n, tl, out_cb, wbufs, fo_group,
                 src_ready=None, wch="wld"):
    nslabs = (fo_n + fo_group - 1) // fo_group
    cols = fo_group * 128
    slab_pe_done = [ctx.val("pe"), ctx.val("pe")]
    wld_t = [0] * nslabs

    gate0 = ("pe", ctx.val("pe"))
    wld_t[0] = emit_weight_load(ctx, wbufs[0], w_dram, fi_n, cols, col0=0,
                                ch=wch, gate=gate0)
    for s in range(nslabs):
        buf = wbufs[s % 2]
        if s + 1 < nslabs:
            gate = ("pe", slab_pe_done[(s + 1) % 2])
            wld_t[s + 1] = emit_weight_load(ctx, wbufs[(s + 1) % 2], w_dram, fi_n,
                                            cols, col0=(s + 1) * cols, ch=wch,
                                            gate=gate)

        def cb(fo_local, banks, s=s):
            out_cb(s * fo_group + fo_local, banks)

        emit_proj(ctx, None, 0, fi_n, buf, cols,
                  list(range(min(fo_group, fo_n - s * fo_group))), tl, cb,
                  wld_target=wld_t[s], src_ready=src_ready, src_tile=src_tile,
                  wld_ch=wch)
        slab_pe_done[s % 2] = ctx.val("pe")



def weights_l0_ffn(inp):
    return {
        "l0f_U1": _np(inp["L0_ffn_U1"]),
        "l0f_V1": _np(inp["L0_ffn_V1"]),
        "l0f_b1": _row(inp["L0_ffn_b1"]),
        "l0f_U2": _np(inp["L0_ffn_U2"]),
        "l0f_V2": _np(inp["L0_ffn_V2"]),
        "l0f_b2": _row(inp["L0_ffn_b2"]),
    }


def emit_l0_ffn(ctx, st, core):
    blk = ctx.block
    x_sb, xln, consts = st["x"], st["xln"], st["consts"]
    dr = st["dram"]

    v1 = ctx.at("v1", [128, 8192], BF16, A_WB0)
    u2 = ctx.at("u2", [128, 8192], BF16, A_WB1)
    u1 = ctx.at("u1", [128, 2048], BF16, P_AX)
    v2 = ctx.at("v2", [128, 2048], BF16, P_AX + 4 * KB)
    b1r = ctx.at("b1r", [1, 4096], BF16, P_AX + 8 * KB)
    b2r = ctx.at("b2r", [1, 1024], BF16, P_AX + 16 * KB)
    qbuf = ctx.at("qbuf", [128, 1024], BF16, OFF_MISC)
    pbuf = ctx.at("pbuf", [128, 4096], BF16, P_KST)
    hbuf = ctx.at("hbuf", [128, 16384], BF16, A_QFM)

    emit_weight_load(ctx, u1, dr["l0f_U1"], 8, 256)
    emit_weight_load(ctx, v1, dr["l0f_V1"], 2, 4096)
    emit_weight_load(ctx, v2, dr["l0f_V2"], 2, 1024)
    emit_row_load(ctx, b1r[0:1, 0:4096], dr["l0f_b1"][0:1, :])
    emit_row_load(ctx, b2r[0:1, 0:1024], dr["l0f_b2"][0:1, :])

    wld_u2 = emit_weight_load(ctx, u2, dr["l0f_U2"], 32, 256)

    emit_ln(ctx, st)
    ln_done = ("ve", ctx.val("ve"))
    wld = wld_u2

    emit_proj(ctx, xln, 0, NF, u1, 256, [0, 1], TL,
              evac_simple(ctx, pbuf, 0, TL, eng="vector"),
              wld_target=wld, src_ready=ln_done, consts=consts)
    p_done = ("ve", ctx.val("ve"))

    for tt in range(NT):
        emit_proj(ctx, pbuf, 0, 2, v1, 4096, list(range(32)), 512,
                  evac_simple(ctx, hbuf, 0, 512, eng="scalar", func=AF.Gelu_apprx_tanh),
                  bias_row=b1r, consts=consts, src_ready=p_done,
                  src_tile=lambda fi, _tt, tt0=tt: pbuf[:, fi * TL + tt0 * 512:
                                                        fi * TL + tt0 * 512 + 512])
        h_done = ("se", ctx.val("se"))
        emit_proj(ctx, hbuf, 0, 32, u2, 256, [0, 1], 512,
                  evac_simple(ctx, qbuf, 0, 512, eng="vector"),
                  src_ready=h_done, consts=consts)
        q_done = ("ve", ctx.val("ve"))

        def mk_out_cb(tt0):
            def cb(fo, banks):
                @blk.vector
                def _(v, fo=fo, banks=banks):
                    ctx.wait_bank_ready(v, banks[0])
                    sl = slice(fo * TL + tt0 * 512, fo * TL + tt0 * 512 + 512)
                    ins = v.tensor_tensor(x_sb[:, sl], x_sb[:, sl],
                                          ctx.ps[banks[0]][:, :], ALU.add)
                    cnt = ctx.inc("ve", ins)
                    ctx.release_bank(banks[0], "ve", cnt)
            return cb

        emit_proj(ctx, qbuf, 0, 2, v2, 1024, list(range(8)), 512,
                  mk_out_cb(tt), bias_row=b2r, consts=consts, src_ready=q_done,
                  src_tile=lambda fi, _tt: qbuf[:, fi * 512:(fi + 1) * 512])



def weights_ternary(inp, L):
    p, o = f"L{L}_ffn_", f"l{L}f_"
    return {
        o + "w1": _tr(inp[p + "w1_w"]),
        o + "w2": _tr(inp[p + "w2_w"]),
        o + "h1": _tr(inp[p + "h1_w"]),
        o + "h2": _tr(inp[p + "h2_w"]),
        o + "w3": _tr(inp[p + "w3_w"]),
        o + "sc": np.concatenate([_col128(inp[p + "w1_s"]), _col128(inp[p + "w2_s"]),
                                  _col128(inp[p + "h1_s"]), _col128(inp[p + "h2_s"]),
                                  _col128(inp[p + "w3_s"])], axis=1),
    }


def weights_l1_ffn(inp):
    return weights_ternary(inp, 1)


def weights_l3_ffn(inp):
    return weights_ternary(inp, 3)


def emit_ternary(ctx, st, core, pref):
    blk = ctx.block
    nc = ctx.nc
    xln, consts, dr = st["xln"], st["consts"], st["dram"]
    x_dram, y_dram = st["x_dram"], st["y_dram"]

    wb = [ctx.at("twb0", [128, 8192], BF16, A_WB0),
          ctx.at("twb1", [128, 8192], BF16, A_WB1)]
    ycb = ctx.at("tyc", [128, 2048], BF16, OFF_MISC)
    scol = ctx.at("tscol", [128, 136], F32, OFF_MISC + 4 * KB)
    harena = ctx.at("tharena", [128, 65536], BF16, OFF_X)
    g1buf = ctx.at("tg1", [128, 16384], F32, OFF_XLN)
    g1bufb = g1buf

    g1_dram = nc.dram_tensor(pref + "g1_spill", [4096, TL], BF16).ap()
    g2_dram = nc.dram_tensor(pref + "g2_spill", [4096, TL], BF16).ap()

    S1, S2, SH1, SH2, S3 = 0, 32, 64, 96, 128

    emit_row_load(ctx, scol[:, :], dr[pref + "sc"][:, :])

    emit_ln(ctx, st)
    ln_done = ("ve", ctx.val("ve"))

    def hsrc(fi, tt):
        return harena[:, fi * TL + tt * 512:fi * TL + (tt + 1) * 512]

    def xlnsrc(fi, tt):
        return xln[:, fi * TL + tt * 512:fi * TL + (tt + 1) * 512]

    wld0 = ctx.val("wld")

    def cbA1(fo, banks):
        @blk.scalar
        def _(s, fo=fo, banks=banks):
            ins = None
            for tt in range(NT):
                ctx.wait_bank_ready(s, banks[tt])
                sl = slice(fo * TL + tt * 512, fo * TL + (tt + 1) * 512)
                ins = s.activation(harena[:, sl], ctx.ps[banks[tt]][:, :], AF.Silu,
                                   scale=scol[:, S1 + fo:S1 + fo + 1])
            cnt = ctx.inc("se", ins)
            for tt in range(NT):
                ctx.release_bank(banks[tt], "se", cnt)

    emit_bigproj(ctx, xlnsrc, 8, dr[pref + "w1"], 32, TL, cbA1, wb, 8,
                 src_ready=ln_done)

    def cbA2(fo, banks):
        @blk.vector
        def _(v, fo=fo, banks=banks):
            ctx.wait(v, "se")
            ins = None
            for tt in range(NT):
                ctx.wait_bank_ready(v, banks[tt])
                sl = slice(fo * TL + tt * 512, fo * TL + (tt + 1) * 512)
                ins = v.scalar_tensor_tensor(harena[:, sl], ctx.ps[banks[tt]][:, :],
                                             scol[:, S2 + fo:S2 + fo + 1],
                                             harena[:, sl], ALU.mult, ALU.mult)
            cnt = ctx.inc("ve", ins)
            for tt in range(NT):
                ctx.release_bank(banks[tt], "ve", cnt)

    emit_bigproj(ctx, xlnsrc, 8, dr[pref + "w2"], 32, TL, cbA2, wb, 8,
                 src_ready=ln_done)
    a_done = ("ve", ctx.val("ve"))

    ycb_state = {"free": None}

    def cbB2(name_dram):
        def cb(fo, banks):
            @blk.scalar
            def _(s, fo=fo, banks=banks):
                if ycb_state["free"] is not None:
                    ctx.wait(s, *ycb_state["free"])
                ins = None
                for tt in range(NT):
                    ctx.wait_bank_ready(s, banks[tt])
                    sl = slice(tt * 512, (tt + 1) * 512)
                    ins = s.activation(ycb[:, sl], ctx.ps[banks[tt]][:, :], AF.Gelu,
                                       scale=scol[:, (SH1 if name_dram is g1_dram
                                                      else SH2) + fo:
                                                  (SH1 if name_dram is g1_dram
                                                   else SH2) + fo + 1])
                cnt = ctx.inc("se", ins)
                for tt in range(NT):
                    ctx.release_bank(banks[tt], "se", cnt)

            @blk.sync
            def _(sy, fo=fo):
                ctx.wait(sy, "se")
                ins = sy.dma_start(out=name_dram[fo * 128:(fo + 1) * 128, :],
                                   in_=ycb[:, 0:TL])
                cnt = ctx.inc("st", ins, 16)
                ycb_state["free"] = ("st", cnt)
        return cb

    emit_bigproj(ctx, hsrc, 32, dr[pref + "h1"], 32, TL, cbB2(g1_dram), wb, 2,
                 src_ready=a_done)
    b_pe_done = ("pe", ctx.val("pe"))

    @blk.gpsimd
    def _(g):
        ctx.wait(g, *b_pe_done)
        ctx.wait(g, "st")
        for c in range(32):
            ins = g.dma_start(out=harena[:, c * TL:(c + 1) * TL],
                              in_=g1_dram[c * 128:(c + 1) * 128, :])
            ctx.inc("g1ld", ins, 16)
    g1_ld = ("g1ld", ctx.val("g1ld"))

    emit_bigproj(ctx, hsrc, 32, dr[pref + "h2"], 32, TL, cbB2(g2_dram), wb, 2,
                 src_ready=g1_ld)
    c_pe_done = ("pe", ctx.val("pe"))

    @blk.gpsimd
    def _(g):
        ctx.wait(g, *c_pe_done)
        ctx.wait(g, "st")
        for c in range(32):
            ins = g.dma_start(out=harena[:, c * TL:(c + 1) * TL],
                              in_=g2_dram[c * 128:(c + 1) * 128, :])
            ctx.inc("g2ld", ins, 16)
    g2_ld = ("g2ld", ctx.val("g2ld"))

    xc_ready = {}
    xc_free = [None, None]

    @blk.sync
    def _(sy):
        for fo in range(2):
            ins = sy.dma_start(out=g1buf[:, fo * TL:(fo + 1) * TL],
                               in_=x_dram[fo * 128:(fo + 1) * 128, :])
            xc_ready[fo] = ("xld", ctx.inc("xld", ins, 16))

    def cbD(fo, banks):
        half = fo % 2

        @blk.vector
        def _(v, fo=fo, banks=banks, half=half):
            ctx.wait(v, *xc_ready[fo])
            ins = None
            for tt in range(NT):
                ctx.wait_bank_ready(v, banks[tt])
                sl = slice(half * TL + tt * 512, half * TL + (tt + 1) * 512)
                ins = v.scalar_tensor_tensor(g1buf[:, sl], ctx.ps[banks[tt]][:, :],
                                             scol[:, S3 + fo:S3 + fo + 1],
                                             g1buf[:, sl], ALU.mult, ALU.add)
            cnt = ctx.inc("ve", ins)
            for tt in range(NT):
                ctx.release_bank(banks[tt], "ve", cnt)

        @blk.sync
        def _(sy, fo=fo, half=half):
            ctx.wait(sy, "ve")
            ins = sy.dma_start(out=y_dram[fo * 128:(fo + 1) * 128, :],
                               in_=g1buf[:, half * TL:(half + 1) * TL])
            cnt = ctx.inc("st", ins, 16)
            if fo + 2 < 8:
                ctx.wait(sy, "st", cnt)
                ins2 = sy.dma_start(out=g1buf[:, half * TL:(half + 1) * TL],
                                    in_=x_dram[(fo + 2) * 128:(fo + 3) * 128, :])
                xc_ready[fo + 2] = ("xld", ctx.inc("xld", ins2, 16))

    emit_bigproj(ctx, hsrc, 32, dr[pref + "w3"], 8, TL, cbD, wb, 2,
                 src_ready=g2_ld)

    @blk.sync
    def _(sy):
        ctx.wait(sy, "st")


def emit_l1_ffn(ctx, st, core):
    emit_ternary(ctx, st, core, "l1f_")


def emit_l3_ffn(ctx, st, core):
    emit_ternary(ctx, st, core, "l3f_")





def emit_kfm_to_dram(ctx, st, w_sb, wld, fo_n, k_dram, core, stage_sb,
                     bias_col=None, scale_col=None, src_ready=None, consts=None,
                     half=None, stage_gate=None, extra=None, extra_ready=None):
    blk = ctx.block
    xln = st["xln"]
    hv = half if half is not None else core % 2

    def cb(fo, banks):
        @blk.scalar
        def _(s, fo=fo, banks=banks):
            ins = None
            for tt in range(NT):
                ctx.wait_bank_ready(s, banks[tt])
                sl = slice(fo * TL + tt * 512, fo * TL + (tt + 1) * 512)
                sc = 1.0 if scale_col is None else scale_col(fo)
                bi = 0.0 if bias_col is None else bias_col(fo)
                ins = s.activation(stage_sb[:, sl], ctx.ps[banks[tt]][:, :], AF.Copy,
                                   scale=sc) if bias_col is None else                     s.activation(stage_sb[:, sl], ctx.ps[banks[tt]][:, :],
                                 AF.Identity, bias=bi, scale=sc)
            cnt = ctx.inc("se", ins)
            for tt in range(NT):
                ctx.release_bank(banks[tt], "se", cnt)

        @blk.sync
        def _(sy, fo=fo):
            ctx.wait(sy, "se")
            o = 0
            for (g0, g1) in core_token_slices_half(hv):
                n = g1 - g0
                ins = sy.dma_start(out=k_dram[fo * 128:(fo + 1) * 128, g0:g1],
                                   in_=stage_sb[:, fo * TL + o:fo * TL + o + n])
                ctx.inc("st", ins, 16)
                o += n

    sr = src_ready
    if extra_ready is not None:
        sr = [src_ready, extra_ready] if src_ready else extra_ready
    emit_proj(ctx, xln, 0, NF, w_sb, fo_n * 128, list(range(fo_n)), TL, cb,
              consts=consts, wld_target=wld, src_ready=sr, extra=extra)


def core_token_slices_half(half):
    return [((2 * b + half) * 512, (2 * b + half) * 512 + 512) for b in range(4)]


def emit_vtok_to_dram(ctx, st, w_sb, wld, dv, v_dram, core, vstg,
                      bias_row=None, src_ready=None, consts=None, half=None,
                      stg_gate=None, extra=None, src_sb=None):
    blk = ctx.block
    xln = src_sb if src_sb is not None else st["xln"]
    hv = half if half is not None else core % 2
    ndv = (dv + 511) // 512
    vstg_free = [stg_gate, stg_gate]

    for tsub in range(TL // 128):
        slot = tsub % 2

        @blk.tensor
        def _(t, tsub=tsub, slot=slot):
            if wld:
                ctx.wait(t, "wld", wld)
            if src_ready is not None:
                for w in (src_ready if isinstance(src_ready, list) else [src_ready]):
                    ctx.wait(t, *w)
            banksv = [6, 7]
            for g in range(ndv):
                ctx.wait_bank_free(t, banksv[g])
            n0 = dv - (ndv - 1) * 512
            ins = None
            for g in range(ndv):
                nn = 512 if g < ndv - 1 else n0
                if bias_row is not None:
                    t.matmul(ctx.ps[banksv[g]][:, 0:nn],
                             consts["ones_row128"][0:1, tsub % 1:tsub % 1 + 128]
                             if False else consts["ones_row_t"][0:1, 0:128],
                             bias_row[0:1, g * 512:g * 512 + nn],
                             start=True, stop=False)
            for fi in range(NF):
                for g in range(ndv):
                    nn = 512 if g < ndv - 1 else n0
                    ins = t.matmul(ctx.ps[banksv[g]][:, 0:nn],
                                   xln[:, fi * TL + tsub * 128:fi * TL + (tsub + 1) * 128],
                                   w_sb[:, fi * dv + g * 512:fi * dv + g * 512 + nn],
                                   start=(fi == 0 and bias_row is None),
                                   stop=(fi == NF - 1 and extra is None))
            if extra is not None:
                zl, zr = extra
                for g in range(ndv):
                    nn = 512 if g < ndv - 1 else n0
                    ins = t.matmul(ctx.ps[banksv[g]][:, 0:nn], zl(tsub),
                                   zr(g, nn), start=False, stop=True)
            cnt = ctx.inc("pe", ins)
            for g in range(ndv):
                ctx.set_bank_ready(banksv[g], "pe", cnt)

        @blk.vector
        def _(v, tsub=tsub, slot=slot):
            banksv = [6, 7]
            if vstg_free[slot] is not None:
                ctx.wait(v, *vstg_free[slot])
            ins = None
            for g in range(ndv):
                nn = 512 if g < ndv - 1 else dv - (ndv - 1) * 512
                ctx.wait_bank_ready(v, banksv[g])
                ins = v.tensor_copy(vstg[:, slot * 1024 + g * 512:
                                         slot * 1024 + g * 512 + nn],
                                    ctx.ps[banksv[g]][:, 0:nn])
            cnt = ctx.inc("ve", ins)
            for g in range(ndv):
                ctx.release_bank(banksv[g], "ve", cnt)

        @blk.sync
        def _(sy, tsub=tsub, slot=slot):
            ctx.wait(sy, "ve")
            g = glob_of(hv, tsub * 128)
            ins = sy.dma_start(out=v_dram[g:g + 128, 0:dv],
                               in_=vstg[:, slot * 1024:slot * 1024 + dv])
            cnt = ctx.inc("st", ins, 16)
            vstg_free[slot] = ("st", cnt)


def glob_of(half, j):
    return (2 * (j // 512) + half) * 512 + j % 512


def emit_attn_core(ctx, st, cfg):
    blk = ctx.block
    nc = ctx.nc
    H, dh = cfg["H"], cfg["dh"]
    k_dram, v_dram = cfg["k_dram"], cfg["v_dram"]
    qfm = cfg["qfm"]
    out_sb = cfg["out_sb"]
    norm = cfg.get("norm", True)
    vbias = cfg.get("v_bias_row")
    pfr = cfg.get("post_factor_row")
    masks = cfg.get("masks")
    kv_ready = cfg["kv_ready"]
    consts = st["consts"]
    nq = TL // 512
    nk = T // 128
    dv = dh

    khb = [ctx.at("khb0", [128, 4096], BF16, A_KH),
           ctx.at("khb1", [128, 4096], BF16, A_KH + 8 * KB)]
    vhb = [ctx.at("vhb0", [128, 2080], BF16, A_VH),
           ctx.at("vhb1", [128, 2080], BF16, A_VH + 4160)]
    st["_khb1"] = khb[1]
    st["_vhb1"] = vhb[1]
    ptile = ctx.at("ptile", [128, 2048], BF16, A_PT)
    fsb = ctx.at("fsb", [128, 512], BF16, A_FSB)
    srow = ctx.at("srow", [1, 1024], F32, A_XTR)
    srow_bf = ctx.at("srow_bf", [1, 1024], BF16, A_SRBF)

    kh_users = [None, None]

    @blk.vector
    def _(v):
        if cfg.get("pre_gate") is not None:
            ctx.wait(v, *cfg["pre_gate"])
        v.memset(vhb[0][:, :], 1.0)
        ins = v.memset(vhb[1][:, :], 1.0)
        ctx.inc("ve", ins)
    vhb_ones = ctx.val("ve")

    for h in range(H):
        slot = h % 2
        @blk.sync
        def _(sy, h=h, slot=slot):
            ctx.wait(sy, "st", kv_ready)
            if cfg.get("kld_gate") is not None:
                ctx.wait(sy, *cfg["kld_gate"])
            if cfg.get("kld_gate2") is not None:
                ctx.wait(sy, *cfg["kld_gate2"])
            ctx.wait(sy, "ve", vhb_ones)
            if kh_users[slot] is not None:
                ctx.wait(sy, *kh_users[slot])
            pb = (h % 2) * 64
            ins = sy.dma_start(out=khb[slot][pb:pb + dh, 0:T],
                               in_=k_dram[h * dh:(h + 1) * dh, :])
            ctx.inc("kld", ins, 16)
            vc = cfg.get("v_col", lambda hh: hh * dv)(h)
            ins = sy.dma_start(
                out=vhb[slot][:, :].rearrange("p (s c) -> p s c", c=65)[:, :, 0:dv],
                in_=v_dram[:, vc:vc + dv].rearrange("(s p) c -> p s c", p=128))
            ctx.inc("kld", ins, 16)
        kld_now = ctx.val("kld")

        hp_gate = None
        if cfg.get("head_pre") is not None:
            hp_gate = cfg["head_pre"](h, slot, kld_now, kh_users[slot])

        for qt in range(nq):
            bO = 4 + (qt % 2)
            qsl = cfg["q_slice"](h, qt)
            kspec = cfg.get("ksub_spec")
            allspecs = [kspec(qt, ksub) if kspec else "full" for ksub in range(nk)]
            live = [i for i, sp in enumerate(allspecs) if sp != "skip"]
            assert live, (qt, "all k masked")
            first_live, last_live = live[0], live[-1]

            for kg in range(nk // 4):
                specs = allspecs[kg * 4:(kg + 1) * 4]
                nsk = [ks for ks in range(4) if specs[ks] != "skip"]
                if not nsk:
                    continue
                allx = len(nsk) == 4

                @blk.tensor
                def _(t, h=h, qt=qt, kg=kg, slot=slot, kld_now=kld_now, bO=bO,
                      nsk=nsk, specs=specs, hp_gate=hp_gate):
                    ctx.wait(t, "kld", kld_now)
                    if hp_gate is not None:
                        ctx.wait(t, *hp_gate)
                    if cfg.get("q_ready") is not None:
                        ctx.wait(t, *cfg["q_ready"])
                    if kg * 4 <= first_live < kg * 4 + 4:
                        ctx.wait_bank_free(t, bO)
                    for ks in nsk:
                        ctx.wait_bank_free(t, ks)
                    ins = None
                    pb = (h % 2) * 64
                    r1fn = cfg.get("ksub_rank1")
                    for ks in nsk:
                        kcol = kg * 512 + ks * 128
                        extras = list(r1fn(h, qt, kg * 4 + ks)) if r1fn else []
                        if specs[ks] != "full":
                            extras.append((consts["ident"][:, :], specs[ks][1]))
                        ins = t.matmul(ctx.psA[:, ks * 512:(ks + 1) * 512],
                                       khb[slot][pb:pb + dh, kcol:kcol + 128], qsl,
                                       start=True, stop=not extras)
                        for j, (el, er) in enumerate(extras):
                            ins = t.matmul(ctx.psA[:, ks * 512:(ks + 1) * 512],
                                           el, er, start=False,
                                           stop=(j == len(extras) - 1))
                    cnt = ctx.inc("pe", ins)
                    for ks in nsk:
                        ctx.set_bank_ready(ks, "pe", cnt)
                    kh_users[slot] = ("pe", cnt)

                @blk.scalar
                def _(s, h=h, qt=qt, kg=kg, nsk=nsk, allx=allx):
                    ctx.wait_bank_ready(s, nsk[0])
                    eb = cfg.get("exp_bias_col")
                    es = cfg.get("exp_scale", 1.0)
                    kw = dict(scale=es(h) if callable(es) else es)
                    if eb is not None:
                        kw["bias"] = eb(h)
                    ins = None
                    if allx:
                        ins = s.activation(ptile[:, :], ctx.psA[:, :], AF.Exp, **kw)
                    else:
                        for ks in nsk:
                            ins = s.activation(ptile[:, ks * 512:(ks + 1) * 512],
                                               ctx.psA[:, ks * 512:(ks + 1) * 512],
                                               AF.Exp, **kw)
                    cnt = ctx.inc("se", ins)
                    for ks in nsk:
                        ctx.release_bank(ks, "se", cnt)
                pt_dep = ("se", ctx.val("se"))

                @blk.tensor
                def _(t, h=h, qt=qt, kg=kg, slot=slot, bO=bO, pt_dep=pt_dep,
                      nsk=nsk):
                    ctx.wait(t, *pt_dep)
                    ins = None
                    for ks in nsk:
                        ksub = kg * 4 + ks
                        lhs = vhb[slot][:, ksub * 65:ksub * 65 + 65]
                        last = (ksub == last_live and vbias is None)
                        ins = t.matmul(ctx.ps[bO][0:65, :], lhs,
                                       ptile[:, ks * 512:(ks + 1) * 512],
                                       start=(ksub == first_live), stop=last)
                    cnt = ctx.inc("pe", ins)
                    kh_users[slot] = ("pe", cnt)
                    for ks in nsk:
                        ctx.release_bank(ks, "pe", cnt)
                    ctx.set_bank_ready(bO, "pe", cnt)

            need_sums = norm or (vbias is not None)
            if need_sums:
                @blk.vector
                def _(v, h=h, qt=qt, bO=bO):
                    ctx.wait_bank_ready(v, bO)
                    v.tensor_copy(srow[0:1, 0:512], ctx.ps[bO][64:65, 0:512])
                    ins = v.tensor_copy(srow_bf[0:1, 0:512], srow[0:1, 0:512])
                    if norm:
                        v.reciprocal(srow[0:1, 512:1024], srow[0:1, 0:512])
                        ins = v.tensor_copy(srow_bf[0:1, 512:1024],
                                            srow[0:1, 512:1024])
                    ctx.inc("ve", ins)
                ve_srow = ctx.val("ve")

                @blk.tensor
                def _(t, h=h, qt=qt, bO=bO):
                    ctx.wait(t, "ve", ve_srow)
                    ins = None
                    if vbias is not None:
                        ins = t.matmul(ctx.ps[bO][0:dv, :],
                                       vbias[0:1, h * dv:(h + 1) * dv],
                                       srow_bf[0:1, 0:512], start=False, stop=True)
                    if norm:
                        ctx.wait_bank_free(t, 6)
                        ins = t.matmul(ctx.ps[6][:, :], consts["ones_row128"][:, :],
                                       srow_bf[0:1, 512:1024], start=True, stop=True)
                    cnt = ctx.inc("pe", ins) if ins is not None else ctx.val("pe")
                    ctx.set_bank_ready(bO, "pe", cnt)
                    if norm:
                        ctx.set_bank_ready(6, "pe", cnt)

            fin_dep = ("pe", ctx.val("pe"))

            @blk.vector
            def _(v, h=h, qt=qt, bO=bO, fin_dep=fin_dep):
                ctx.wait(v, *fin_dep)
                ch, pb = cfg["out_base"](h)
                osl = out_sb[pb:pb + dv, ch * TL + qt * 512:ch * TL + (qt + 1) * 512]
                if norm:
                    v.tensor_copy(fsb[:, :], ctx.ps[6][:, :])
                    ins = v.tensor_tensor(osl, ctx.ps[bO][0:dv, :], fsb[0:dv, :],
                                          ALU.mult)
                else:
                    ins = v.tensor_copy(osl, ctx.ps[bO][0:dv, :])
                if pfr is not None:
                    ins = v.tensor_tensor(osl, osl, pfr(h, qt), ALU.mult)
                cnt = ctx.inc("ve", ins)
                ctx.release_bank(bO, "ve", cnt)
                if norm:
                    ctx.release_bank(6, "ve", cnt)



def weights_l3_attn_a(inp):
    def circ(c):
        c = np.asarray(c, dtype=np.float32)
        return np.ascontiguousarray(np.stack([np.roll(c, j) for j in range(D)]))
    return {
        "l3a_Cq": circ(inp["L3_attn_a_wq_c"]),
        "l3a_Ck": circ(inp["L3_attn_a_wk_c"]),
        "l3a_Cv": circ(inp["L3_attn_a_wv_c"]),
        "l3a_Co": circ(inp["L3_attn_a_wo_c"]),
        "l3a_bq": _col128(inp["L3_attn_a_wq_b"]),
        "l3a_bk": _col128(inp["L3_attn_a_wk_b"]),
        "l3a_bv": _row(inp["L3_attn_a_wv_b"]),
        "l3a_bo": _row(inp["L3_attn_a_wo_b"]),
    }


DEBUG_TAPS = False


def emit_l3_attn_a(ctx, st, core):
    blk = ctx.block
    nc = ctx.nc
    dr = st["dram"]
    consts = st["consts"]
    x_sb, xln = st["x"], st["xln"]

    k_dram = nc.dram_tensor("l3a_K", [D, T], BF16).ap()
    v_dram = nc.dram_tensor("l3a_V", [T, D], BF16).ap()

    wk = ctx.at("wk", [128, 8192], BF16, A_WB0)
    wv = ctx.at("wv", [128, 8192], BF16, A_WB1)
    stage = ctx.at("kvstage", [128, 16384], BF16, A_QFM)
    vstg = ctx.at("vstg", [128, 2048], BF16, A_VSTG)
    bqc = ctx.at("bqc", [128, 8], F32, A_MISC)
    bkc = ctx.at("bkc", [128, 8], F32, A_MISC + 64)
    bvr = ctx.at("bvr", [1, 1024], BF16, A_MISC + 128)
    bor = ctx.at("bor", [1, 1024], BF16, A_MISC + 2176 + 64)

    emit_weight_load(ctx, wk, dr["l3a_Ck"], 8, 1024)
    emit_weight_load(ctx, wv, dr["l3a_Cv"], 8, 1024)
    emit_row_load(ctx, bqc[:, :], dr["l3a_bq"][:, :])
    emit_row_load(ctx, bkc[:, :], dr["l3a_bk"][:, :])
    emit_row_load(ctx, bvr[0:1, :], dr["l3a_bv"][0:1, :])
    wld0 = emit_row_load(ctx, bor[0:1, :], dr["l3a_bo"][0:1, :])

    peer = core ^ 1
    emit_load_x(ctx, x_sb, st["xp_dram"])
    emit_ln(ctx, st, src_wait=("ld", ctx.val("ld")))
    ln1 = ("ve", ctx.val("ve"))
    emit_kfm_to_dram(ctx, st, wk, wld0, NF, k_dram, core, stage,
                     bias_col=lambda fo: bkc[:, fo:fo + 1], src_ready=ln1,
                     consts=consts, half=peer % 2)
    emit_vtok_to_dram(ctx, st, wv, wld0, D, v_dram, core, vstg,
                      bias_row=None, src_ready=ln1, consts=consts, half=peer % 2)
    p1_pe = ("pe", ctx.val("pe"))
    p1_st = ctx.val("st")

    emit_load_x2(ctx, x_sb, st["x_dram"], gate=p1_pe)
    emit_ln(ctx, st, src_wait=[("ld", ctx.val("ld")), p1_pe,
                               ("st", ctx.val("st"))])
    ln2 = ("ve", ctx.val("ve"))
    emit_kfm_to_dram(ctx, st, wk, wld0, NF, k_dram, core, stage,
                     bias_col=lambda fo: bkc[:, fo:fo + 1], src_ready=ln2,
                     consts=consts, half=core % 2, stage_gate=("st", p1_st))
    emit_vtok_to_dram(ctx, st, wv, wld0, D, v_dram, core, vstg,
                      bias_row=None, src_ready=ln2, consts=consts, half=core % 2,
                      stg_gate=("st", p1_st))
    kv_done_st = ctx.val("st")
    p2_pe = ("pe", ctx.val("pe"))

    wq = ctx.at("wq", [128, 8192], BF16, A_WB0)
    wldq = emit_weight_load(ctx, wq, dr["l3a_Cq"], 8, 1024, gate=p2_pe)
    qfm = ctx.at("qfm", [128, 16384], BF16, A_QFM)

    def qcb2(fo, banks):
        @blk.scalar
        def _(s, fo=fo, banks=banks):
            ins = None
            for tt in range(NT):
                ctx.wait_bank_ready(s, banks[tt])
                sl = slice(fo * TL + tt * 512, fo * TL + (tt + 1) * 512)
                ins = s.activation(qfm[:, sl], ctx.ps[banks[tt]][:, :], AF.Identity,
                                   bias=bqc[:, fo:fo + 1])
            cnt = ctx.inc("se", ins)
            for tt in range(NT):
                ctx.release_bank(banks[tt], "se", cnt)

    emit_proj(ctx, xln, 0, NF, wq, 1024, list(range(8)), TL, qcb2,
              consts=consts, wld_target=wldq, src_ready=("st", kv_done_st))

    attnout = ctx.at("attnout", [128, 16384], BF16, OFF_XLN)

    cfg = {
        "H": 16, "dh": 64, "k_dram": k_dram, "v_dram": v_dram,
        "exp_scale": 64.0 ** -0.5, "norm": True, "v_bias_row": bvr,
        "out_sb": attnout, "qfm": qfm, "kv_ready": kv_done_st,
        "q_slice": lambda h, qt: qfm[(h % 2) * 64:(h % 2) * 64 + 64,
                                     (h // 2) * TL + qt * 512:(h // 2) * TL + (qt + 1) * 512],
        "out_base": lambda h: (h // 2, (h % 2) * 64),
    }
    emit_attn_core(ctx, st, cfg)
    att_done = ("ve", ctx.val("ve"))

    if DEBUG_TAPS:
        blk2 = ctx.block
        nc2 = ctx.nc
        dbg_khb = nc2.dram_tensor("dbg_khb", [128, T], F32, kind="ExternalOutput").ap()
        dbg_vhb = nc2.dram_tensor("dbg_vhb", [128, 2080], F32, kind="ExternalOutput").ap()

        @blk2.gpsimd
        def _(g):
            ctx.wait(g, "ve")
            ins = g.dma_start(out=dbg_khb[:, :], in_=st["_khb1"][:, :])
            ctx.inc("st", ins, 16)
            ins = g.dma_start(out=dbg_vhb[:, :], in_=st["_vhb1"][:, :])
            ctx.inc("st", ins, 16)

        dbg_k = nc2.dram_tensor("dbg_k", [D, T], F32, kind="ExternalOutput").ap()
        dbg_v = nc2.dram_tensor("dbg_v", [T, D], F32, kind="ExternalOutput").ap()
        dbg_q = nc2.dram_tensor("dbg_q", [D, TL], F32, kind="ExternalOutput").ap()
        dbg_att = nc2.dram_tensor("dbg_att", [D, TL], F32, kind="ExternalOutput").ap()

        @blk2.gpsimd
        def _(g):
            ctx.wait(g, "ve")
            ins = g.dma_start(out=dbg_k[:, :], in_=k_dram[:, :])
            ctx.inc("st", ins, 16)
            ins = g.dma_start(out=dbg_v[:, :], in_=v_dram[:, :])
            ctx.inc("st", ins, 16)
            for c in range(NF):
                ins = g.dma_start(out=dbg_q[c * 128:(c + 1) * 128, :],
                                  in_=qfm[:, c * TL:(c + 1) * TL])
                ctx.inc("st", ins, 16)
                ins = g.dma_start(out=dbg_att[c * 128:(c + 1) * 128, :],
                                  in_=attnout[:, c * TL:(c + 1) * TL])
                ctx.inc("st", ins, 16)

    wo = ctx.at("wo", [128, 8192], BF16, A_WB1)
    wldo = emit_weight_load(ctx, wo, dr["l3a_Co"], 8, 1024, gate=p2_pe)
    emit_proj(ctx, attnout, 0, NF, wo, 1024, list(range(8)), TL,
              evac_resid(ctx, x_sb), bias_row=bor, consts=consts,
              wld_target=wldo, src_ready=att_done)


def emit_load_x2(ctx, x_sb, x_dram, tl=TL, gate=None):
    blk = ctx.block

    @blk.sync
    def _(sy):
        if gate:
            ctx.wait(sy, *gate)
        ctx.wait(sy, "ve")
        for c in range(NF):
            ins = sy.dma_start(out=x_sb[:, c * tl:(c + 1) * tl],
                               in_=x_dram[c * 128:(c + 1) * 128, :])
            ctx.inc("ld", ins, 16)



def weights_l2_attn_b(inp):
    e16 = np.zeros((1024, 16), dtype=np.float32)
    for h in range(16):
        e16[h * 64:(h + 1) * 64, h] = 1.0
    e16T = np.ascontiguousarray(e16.T)
    return {
        "l2b_Wq": _np(inp["L2_attn_b_Wq"]),
        "l2b_Wk": _np(inp["L2_attn_b_Wk"]),
        "l2b_Wv": _np(inp["L2_attn_b_Wv"]),
        "l2b_Wo": _np(inp["L2_attn_b_Wo"]),
        "l2b_bq": _row(inp["L2_attn_b_bq"]),
        "l2b_bk": _row(inp["L2_attn_b_bk"]),
        "l2b_bv": _row(inp["L2_attn_b_bv"]),
        "l2b_bo": _row(inp["L2_attn_b_bo"]),
        "l2b_e16": e16,
        "l2b_e16T": e16T,
    }


def emit_headln_fm(ctx, raw_sb, out_write, e16, e16T, hstat, hscr, rbbuf, sq_scr, consts):
    blk = ctx.block
    MU, E2 = 0, TL

    sq_slots = [None]
    for c in range(NF):
        @blk.scalar
        def _(s, c=c):
            if sq_slots[0] is not None:
                ctx.wait(s, *sq_slots[0])
            ctx.wait(s, "ve")
            ins = s.activation(sq_scr[:, 0:TL],
                               raw_sb[:, c * TL:(c + 1) * TL], AF.Square)
            ctx.inc("se", ins)
        se_now = ctx.val("se")

        @blk.tensor
        def _(t, c=c, se_now=se_now):
            if c == 0:
                for bk in range(8):
                    ctx.wait_bank_free(t, bk)
            ctx.wait(t, "se", se_now)
            ins = None
            for tt in range(NT):
                sl = slice(c * TL + tt * 512, c * TL + (tt + 1) * 512)
                t.matmul(ctx.ps[tt][0:16, :], e16[:, c * 16:(c + 1) * 16],
                         raw_sb[:, sl], start=(c == 0), stop=(c == NF - 1))
                sl2 = slice(tt * 512, (tt + 1) * 512)
                ins = t.matmul(ctx.ps[4 + tt][0:16, :], e16[:, c * 16:(c + 1) * 16],
                               sq_scr[:, sl2], start=(c == 0), stop=(c == NF - 1))
            cnt = ctx.inc("pe", ins)
            sq_slots[0] = ("pe", cnt)
            if c == NF - 1:
                for tt in range(NT):
                    ctx.set_bank_ready(tt, "pe", cnt)
                    ctx.set_bank_ready(4 + tt, "pe", cnt)

    @blk.vector
    def _(v):
        ctx.wait_bank_ready(v, 0)
        ins = None
        for tt in range(NT):
            v.tensor_copy(hstat[0:16, MU + tt * 512:MU + (tt + 1) * 512],
                          ctx.ps[tt][0:16, :])
            ins = v.tensor_copy(hstat[0:16, E2 + tt * 512:E2 + (tt + 1) * 512],
                                ctx.ps[4 + tt][0:16, :])
        cnt = ctx.inc("ve", ins)
        for bk in range(8):
            ctx.release_bank(bk, "ve", cnt)

    @blk.vector
    def _(v):
        ctx.wait(v, "ve")
        v.tensor_scalar_mul(hstat[0:16, MU:MU + TL], hstat[0:16, MU:MU + TL], 1.0 / 64)
        v.tensor_scalar_mul(hstat[0:16, E2:E2 + TL], hstat[0:16, E2:E2 + TL], 1.0 / 64)
        ins = None
        for i in range(NT):
            m = slice(MU + i * 512, MU + (i + 1) * 512)
            e = slice(E2 + i * 512, E2 + (i + 1) * 512)
            v.tensor_tensor(hscr[0:16, 0:512], hstat[0:16, m],
                            hstat[0:16, m], ALU.mult)
            ins = v.tensor_tensor(hstat[0:16, e], hstat[0:16, e],
                                  hscr[0:16, 0:512], ALU.subtract)
        ctx.inc("ve", ins)

    @blk.scalar
    def _(s):
        ctx.wait(s, "ve")
        s.activation(hstat[0:16, E2:E2 + TL], hstat[0:16, E2:E2 + TL], AF.Ln,
                     bias=consts["cpool"][0:16, 0:1])
        ins = s.activation(hstat[0:16, E2:E2 + TL], hstat[0:16, E2:E2 + TL],
                           AF.Exp, scale=-0.5)
        ctx.inc("se", ins)
    stats_done = ctx.val("se")

    for tt in range(NT):
        @blk.vector
        def _(v, tt=tt):
            ctx.wait(v, "se", stats_done)
            if tt > 0:
                ctx.wait(v, "pe")
            v.tensor_copy(rbbuf[0:16, 0:512], hstat[0:16, MU + tt * 512:MU + (tt + 1) * 512])
            ins = v.tensor_copy(rbbuf[0:16, 512:1024],
                                hstat[0:16, E2 + tt * 512:E2 + (tt + 1) * 512])
            ctx.inc("ve", ins)
        rb_done = ctx.val("ve")

        for c in range(NF):
            bM = (2 * c) % 4
            bR = (2 * c) % 4 + 1

            @blk.tensor
            def _(t, c=c, tt=tt, bM=bM, bR=bR, rb_done=rb_done):
                ctx.wait(t, "ve", rb_done)
                ctx.wait_bank_free(t, bM)
                ctx.wait_bank_free(t, bR)
                t.matmul(ctx.ps[bM][:, :], e16T[:, c * 128:(c + 1) * 128],
                         rbbuf[0:16, 0:512], start=True, stop=True)
                ins = t.matmul(ctx.ps[bR][:, :], e16T[:, c * 128:(c + 1) * 128],
                               rbbuf[0:16, 512:1024], start=True, stop=True)
                cnt = ctx.inc("pe", ins)
                ctx.set_bank_ready(bM, "pe", cnt)
                ctx.set_bank_ready(bR, "pe", cnt)

            out_write(c, tt, bM, bR)


def _l2b_norm_write(ctx, dst):
    blk = ctx.block

    def out_write(c, tt, bM, bR):
        @blk.vector
        def _(v, c=c, tt=tt, bM=bM, bR=bR):
            ctx.wait_bank_ready(v, bM)
            sl = slice(c * TL + tt * 512, c * TL + (tt + 1) * 512)
            v.tensor_tensor(dst[:, sl], dst[:, sl], ctx.ps[bM][:, :], ALU.subtract)
            ins = v.tensor_tensor(dst[:, sl], dst[:, sl], ctx.ps[bR][:, :], ALU.mult)
            cnt = ctx.inc("ve", ins)
            ctx.release_bank(bM, "ve", cnt)
            ctx.release_bank(bR, "ve", cnt)
    return out_write


L2B_VPART = [99]


def emit_l2b_v(ctx, st, w_sb, wld, v_dram, half, vstg, vsq, bias_row, consts,
               src_ready, stg_gate):
    blk = ctx.block
    xln = st["xln"]
    vstg_free = [stg_gate, stg_gate]

    for tsub in range(TL // 128):
        slot = tsub % 2

        @blk.tensor
        def _(t, tsub=tsub):
            ctx.wait(t, "wld", wld)
            ctx.wait(t, *src_ready)
            for g in range(2):
                ctx.wait_bank_free(t, 6 + g)
            ins = None
            import os
            use_bias = os.environ.get("L2B_NOBIAS") is None
            if use_bias:
                for g in range(2):
                    t.matmul(ctx.ps[6 + g][:, :], consts["ones_row_t"][0:1, 0:128],
                             bias_row[0:1, g * 512:(g + 1) * 512], start=True, stop=False)
            for fi in range(NF):
                for g in range(2):
                    ins = t.matmul(ctx.ps[6 + g][:, :],
                                   xln[:, fi * TL + tsub * 128:fi * TL + (tsub + 1) * 128],
                                   w_sb[:, fi * 1024 + g * 512:fi * 1024 + (g + 1) * 512],
                                   start=(fi == 0 and not use_bias),
                                   stop=(fi == NF - 1))
            cnt = ctx.inc("pe", ins)
            for g in range(2):
                ctx.set_bank_ready(6 + g, "pe", cnt)

        @blk.vector
        def _(v, tsub=tsub, slot=slot):
            if vstg_free[slot] is not None:
                ctx.wait(v, *vstg_free[slot])
            ctx.wait_bank_ready(v, 6)
            ins = None
            for g in range(2):
                ins = v.tensor_copy(vstg[:, slot * 1024 + g * 512:
                                         slot * 1024 + (g + 1) * 512],
                                    ctx.ps[6 + g][:, :])
            cnt = ctx.inc("ve", ins)
            for g in range(2):
                ctx.release_bank(6 + g, "ve", cnt)

        if L2B_VPART[0] <= 1:
            @blk.sync
            def _(sy, tsub=tsub, slot=slot):
                ctx.wait(sy, "ve")
                g = glob_of(half, tsub * 128)
                ins = sy.dma_start(out=v_dram[g:g + 128, 0:D],
                                   in_=vstg[:, slot * 1024:(slot + 1) * 1024])
                cnt = ctx.inc("st", ins, 16)
                vstg_free[slot] = ("st", cnt)
            continue

        @blk.vector
        def _(v, slot=slot):
            ins = v.tensor_tensor(vsq[:, slot * 1024:(slot + 1) * 1024],
                                  vstg[:, slot * 1024:(slot + 1) * 1024],
                                  vstg[:, slot * 1024:(slot + 1) * 1024], ALU.mult)
            ctx.inc("se", ins)

        if L2B_VPART[0] <= 2:
            @blk.sync
            def _(sy, tsub=tsub, slot=slot):
                ctx.wait(sy, "se")
                g = glob_of(half, tsub * 128)
                ins = sy.dma_start(out=v_dram[g:g + 128, 0:D],
                                   in_=vstg[:, slot * 1024:(slot + 1) * 1024])
                cnt = ctx.inc("st", ins, 16)
                vstg_free[slot] = ("st", cnt)
            continue

        @blk.vector
        def _(v, slot=slot):
            ctx.wait(v, "se")
            base = slot * 1024
            osp = (1 - slot) * 1024
            vs = st["_l2b_vstat"]
            o = slot * 32

            def spacer():
                v.tensor_copy(vsq[:, osp:osp + 512], vsq[:, osp:osp + 512])
            v.tensor_reduce(vs[:, o:o + 16],
                            vstg[:, base:base + 1024].rearrange("p (h d) -> p h d", d=64),
                            AX.X, ALU.add)
            v.tensor_reduce(vs[:, o + 16:o + 32],
                            vsq[:, base:base + 1024].rearrange("p (h d) -> p h d", d=64),
                            AX.X, ALU.add)
            spacer()
            v.tensor_scalar_mul(vs[:, o:o + 16], vs[:, o:o + 16], 1.0 / 64)
            v.tensor_scalar_mul(vs[:, o + 16:o + 32], vs[:, o + 16:o + 32], 1.0 / 64)
            spacer()
            v.tensor_tensor(vs[:, 64:80], vs[:, o:o + 16], vs[:, o:o + 16], ALU.mult)
            spacer()
            ins = v.tensor_tensor(vs[:, o + 16:o + 32], vs[:, o + 16:o + 32],
                                  vs[:, 64:80], ALU.subtract)
            ctx.inc("ve", ins)

        @blk.scalar
        def _(s, slot=slot):
            vs = st["_l2b_vstat"]
            o = slot * 32
            ctx.wait(s, "ve")
            import os as _os4
            if _os4.environ.get("L2B_NORSTD"):
                ins = s.activation(vs[:, o + 16:o + 32], vs[:, o + 16:o + 32],
                                   AF.Copy)
            else:
                s.activation(vs[:, o + 16:o + 32], vs[:, o + 16:o + 32], AF.Ln,
                             bias=consts["cpool"][:, 0:1])
                ins = s.activation(vs[:, o + 16:o + 32], vs[:, o + 16:o + 32],
                                   AF.Exp, scale=-0.5)
            ctx.inc("se", ins)

        @blk.vector
        def _(v, slot=slot):
            vs = st["_l2b_vstat"]
            o = slot * 32
            base = slot * 1024
            ctx.wait(v, "se")
            v3 = vstg[:, base:base + 1024].rearrange("p (h d) -> p h d", d=64)
            q3 = vsq[:, base:base + 1024].rearrange("p (h d) -> p h d", d=64)
            import os as _os3
            dump = _os3.environ.get("L2B_DUMPMU")
            if dump == "mu":
                ins = v.tensor_scalar(v3, vs[:, o:o + 16].unsqueeze(2)
                                      .broadcast_to([128, 16, 64]), 1.0, None, ALU.mult)
            elif dump == "rs":
                ins = v.tensor_scalar(v3, vs[:, o + 16:o + 32].unsqueeze(2)
                                      .broadcast_to([128, 16, 64]), 1.0, None, ALU.mult)
            else:
                v.tensor_tensor(q3, v3,
                                vs[:, o:o + 16].unsqueeze(2).broadcast_to([128, 16, 64]),
                                ALU.subtract)
                ins = v.tensor_tensor(v3, q3,
                                      vs[:, o + 16:o + 32].unsqueeze(2)
                                      .broadcast_to([128, 16, 64]),
                                      ALU.mult)
            ctx.inc("ve", ins)

        @blk.sync
        def _(sy, tsub=tsub, slot=slot):
            ctx.wait(sy, "ve")
            g = glob_of(half, tsub * 128)
            ins = sy.dma_start(out=v_dram[g:g + 128, 0:D],
                               in_=vstg[:, slot * 1024:(slot + 1) * 1024])
            cnt = ctx.inc("st", ins, 16)
            vstg_free[slot] = ("st", cnt)


L2B_STAGE = [99]


def emit_l2_attn_b(ctx, st, core):
    blk = ctx.block
    nc = ctx.nc
    dr = st["dram"]
    consts = st["consts"]
    x_sb, xln = st["x"], st["xln"]

    k_dram = nc.dram_tensor("l2b_K", [D, T], BF16).ap()
    v_dram = nc.dram_tensor("l2b_V", [T, D], BF16).ap()

    wA = ctx.at("l2b_wA", [128, 8192], BF16, A_WB0)
    wB = ctx.at("l2b_wB", [128, 8192], BF16, A_WB1)
    stage = ctx.at("l2b_stage", [128, 16384], BF16, A_QFM)
    e16 = ctx.at("l2b_e16", [128, 128], BF16, A_MISC)
    e16T = ctx.at("l2b_e16T", [16, 1024], BF16, A_MISC + 256)
    bqr = ctx.at("l2b_bq", [1, 1024], BF16, A_MISC + 2304)
    bkr = ctx.at("l2b_bk", [1, 1024], BF16, A_MISC + 4352)
    bvr = ctx.at("l2b_bv", [1, 1024], BF16, A_MISC + 6400)
    hstat = ctx.at("l2b_hstat", [16, 2 * TL], F32, P_AX)
    hscr = ctx.at("l2b_hscr", [16, 512], F32, P_AX + 16 * KB)
    rbbuf = ctx.at("l2b_rbbuf", [16, 1024], BF16, P_AX + 18 * KB)
    vstat = ctx.at("l2b_vstat", [128, 80], F32, A_MISC + 8448)
    vstg = ctx.at("l2b_vstg", [128, 2048], BF16, A_VSTG)
    vsq = ctx.at("l2b_vsq", [128, 2048], BF16, P_KST)
    hsq = ctx.at("l2b_hsq", [128, 2048], BF16, P_KST + 4 * KB)
    st["_l2b_vstat"] = vstat

    emit_weight_load(ctx, e16, dr["l2b_e16"], 8, 16)
    emit_row_load(ctx, e16T[0:16, :], dr["l2b_e16T"][0:16, :])
    emit_row_load(ctx, bqr[0:1, :], dr["l2b_bq"][0:1, :])
    emit_row_load(ctx, bkr[0:1, :], dr["l2b_bk"][0:1, :])
    emit_row_load(ctx, bvr[0:1, :], dr["l2b_bv"][0:1, :])
    wldk = emit_weight_load(ctx, wA, dr["l2b_Wk"], 8, 1024)
    wldv = emit_weight_load(ctx, wB, dr["l2b_Wv"], 8, 1024)

    for phase, half in ((0, (core ^ 1) % 2), (1, core % 2)):
        if phase == 0:
            emit_load_x(ctx, x_sb, st["xp_dram"])
            emit_ln(ctx, st, src_wait=("ld", ctx.val("ld")))
        else:
            gate = ("pe", ctx.val("pe"))
            emit_load_x2(ctx, x_sb, st["x_dram"], gate=gate)
            emit_ln(ctx, st, src_wait=[("ld", ctx.val("ld")), gate,
                                       ("st", ctx.val("st"))])
        lnd = ("ve", ctx.val("ve"))
        stg_gate = ("st", ctx.val("st"))

        import os as _os2
        if _os2.environ.get("L2B_NOK") is None:
            emit_proj(ctx, xln, 0, NF, wA, 1024, list(range(8)), TL,
                      evac_simple(ctx, stage, 0, TL, eng="vector"),
                      bias_row=bkr[0:1, :], consts=consts,
                      wld_target=wldk, src_ready=lnd)
        if L2B_STAGE[0] <= 1:
            return
        import os as _os
        if _os.environ.get("L2B_NOHEADLN") is None:
            emit_headln_fm(ctx, stage, _l2b_norm_write(ctx, stage), e16, e16T,
                           hstat, hscr, rbbuf, st["sq"], consts)
        if L2B_STAGE[0] <= 2:
            return

        @blk.sync
        def _(sy, half=half):
            if _os2.environ.get("L2B_NOK") is not None:
                return
            ctx.wait(sy, "ve")
            if stg_gate[1]:
                ctx.wait(sy, *stg_gate)
            for c in range(NF):
                o = 0
                for (g0, g1) in core_token_slices_half(half):
                    n = g1 - g0
                    ins = sy.dma_start(out=k_dram[c * 128:(c + 1) * 128, g0:g1],
                                       in_=stage[:, c * TL + o:c * TL + o + n])
                    ctx.inc("st", ins, 16)
                    o += n

        if L2B_STAGE[0] <= 3:
            return
        emit_l2b_v(ctx, st, wB, wldv, v_dram, half, vstg, vsq,
                   bvr[0:1, :], consts, lnd, stg_gate)
        if L2B_STAGE[0] <= 4:
            return

    kv_done_st = ctx.val("st")
    p2_pe = ("pe", ctx.val("pe"))

    wq = ctx.at("l2b_wq", [128, 8192], BF16, A_WB0)
    wldq = emit_weight_load(ctx, wq, dr["l2b_Wq"], 8, 1024, gate=p2_pe)
    emit_proj(ctx, xln, 0, NF, wq, 1024, list(range(8)), TL,
              evac_simple(ctx, stage, 0, TL, eng="vector"),
              bias_row=bqr[0:1, :], consts=consts,
              wld_target=wldq, src_ready=("st", kv_done_st))
    emit_headln_fm(ctx, stage, _l2b_norm_write(ctx, stage), e16, e16T,
                   hstat, hscr, rbbuf, hsq, consts)
    if L2B_STAGE[0] <= 5:
        return
    qfm = stage
    q_done = ("ve", ctx.val("ve"))

    attnout = ctx.at("l2b_attnout", [128, 16384], BF16, OFF_XLN)
    cfg = {
        "H": 16, "dh": 64, "k_dram": k_dram, "v_dram": v_dram,
        "exp_scale": 0.125, "norm": True, "v_bias_row": None,
        "out_sb": attnout, "qfm": qfm, "kv_ready": kv_done_st,
        "q_slice": lambda h, qt: qfm[(h % 2) * 64:(h % 2) * 64 + 64,
                                     (h // 2) * TL + qt * 512:(h // 2) * TL + (qt + 1) * 512],
        "out_base": lambda h: (h // 2, (h % 2) * 64),
        "q_ready": q_done,
        "kld_gate": q_done,
        "pre_gate": ("pe", ctx.val("pe")),
    }
    emit_attn_core(ctx, st, cfg)
    att_done = ("ve", ctx.val("ve"))
    if DEBUG_TAPS:
        nc2 = ctx.nc
        dbg_k = nc2.dram_tensor("dbg_k", [D, T], F32, kind="ExternalOutput").ap()
        dbg_v = nc2.dram_tensor("dbg_v", [T, D], F32, kind="ExternalOutput").ap()
        dbg_q = nc2.dram_tensor("dbg_q", [D, TL], F32, kind="ExternalOutput").ap()
        dbg_att = nc2.dram_tensor("dbg_att", [D, TL], F32, kind="ExternalOutput").ap()

        @blk.gpsimd
        def _(g):
            ctx.wait(g, "ve")
            ins = g.dma_start(out=dbg_k[:, :], in_=k_dram[:, :])
            ctx.inc("st", ins, 16)
            ins = g.dma_start(out=dbg_v[:, :], in_=v_dram[:, :])
            ctx.inc("st", ins, 16)
            for c in range(NF):
                ins = g.dma_start(out=dbg_q[c * 128:(c + 1) * 128, :],
                                  in_=qfm[:, c * TL:(c + 1) * TL])
                ctx.inc("st", ins, 16)
                ins = g.dma_start(out=dbg_att[c * 128:(c + 1) * 128, :],
                                  in_=attnout[:, c * TL:(c + 1) * TL])
                ctx.inc("st", ins, 16)
    if L2B_STAGE[0] <= 6:
        return

    wo = ctx.at("l2b_wo", [128, 8192], BF16, A_WB1)
    bor = ctx.at("l2b_bo", [1, 1024], BF16, A_VH)
    emit_row_load(ctx, bor[0:1, :], dr["l2b_bo"][0:1, :], gate=att_done)
    wldo = emit_weight_load(ctx, wo, dr["l2b_Wo"], 8, 1024, gate=p2_pe)
    emit_proj(ctx, attnout, 0, NF, wo, 1024, list(range(8)), TL,
              evac_resid(ctx, x_sb), bias_row=bor[0:1, :], consts=consts,
              wld_target=wldo, src_ready=att_done)



TWO_PI = 6.283185307179586
CW1 = 6.28125
CW2 = float(np.float32(np.float64(TWO_PI) - np.float64(np.float32(CW1))))
CW3 = float(np.float64(TWO_PI) - np.float64(np.float32(CW1)) - np.float64(np.float32(CW2)))
HALF_PI = 1.5707963267948966


def weights_l0_attn_b(inp):
    return {
        "l0b_WqT": _tr(inp["L0_attn_b_Wq"]),
        "l0b_WkT": _tr(inp["L0_attn_b_Wk"]),
        "l0b_WvT": _tr(inp["L0_attn_b_Wv"]),
        "l0b_WoT": _tr(inp["L0_attn_b_Wo"]),
        "l0b_wav": _col128(inp["L0_attn_b_wavelengths"]),
        "l0b_phase": _col128(inp["L0_attn_b_phase_bias"]),
        "l0b_bq": _col128(inp["L0_attn_b_bq"]),
        "l0b_bk": _col128(inp["L0_attn_b_bk"]),
        "l0b_bv": _row(inp["L0_attn_b_bv"]),
        "l0b_bo": _row(inp["L0_attn_b_bo"]),
    }


def emit_trig_from_theta(ctx, dst_bf, dst_off, thf, scr, kf, iscr, off, gate=None):
    blk = ctx.block

    @blk.vector
    def _(v):
        ctx.wait(v, "se")
        v.tensor_scalar(scr[:, :], thf[:, :], 1.0 / TWO_PI, off / TWO_PI,
                        ALU.mult, ALU.add)
        v.tensor_copy(iscr[:, :], scr[:, :])
        v.tensor_copy(kf[:, :], iscr[:, :])
        v.scalar_tensor_tensor(scr[:, :], kf[:, :], -CW1, thf[:, :],
                               ALU.mult, ALU.add)
        if off != 0.0:
            v.tensor_scalar_add(scr[:, :], scr[:, :], off)
        v.scalar_tensor_tensor(scr[:, :], kf[:, :], -CW2, scr[:, :],
                               ALU.mult, ALU.add)
        ins = v.scalar_tensor_tensor(scr[:, :], kf[:, :], -CW3, scr[:, :],
                                     ALU.mult, ALU.add)
        ctx.inc("ve", ins)

    @blk.scalar
    def _(s):
        ctx.wait(s, "ve")
        if gate is not None:
            ctx.wait(s, *gate)
        ins = s.activation(dst_bf[:, dst_off:dst_off + 2048], scr[:, :], AF.Sin)
        ctx.inc("se", ins)


def emit_l0_attn_b(ctx, st, core):
    blk = ctx.block
    nc = ctx.nc
    dr = st["dram"]
    consts = st["consts"]
    x_sb, xln = st["x"], st["xln"]

    k_dram = nc.dram_tensor("l0b_K", [256, T], BF16).ap()
    v_dram = nc.dram_tensor("l0b_V", [T, 128], BF16).ap()

    wq = ctx.at("l0b_wq", [128, 1024], BF16, A_WB0)
    wk = ctx.at("l0b_wk", [128, 1024], BF16, A_WB0 + 2 * KB)
    wv = ctx.at("l0b_wv", [128, 1024], BF16, A_WB0 + 4 * KB)
    wo = ctx.at("l0b_wo", [128, 2048], BF16, A_WB0 + 6 * KB)
    cols = ctx.at("l0b_cols", [128, 16], F32, A_MISC)
    bvr = ctx.at("l0b_bv", [1, 128], BF16, A_MISC + 128)
    bor = ctx.at("l0b_bo", [1, 1024], BF16, A_MISC + 512)
    colscr = ctx.at("l0b_colscr", [128, 512], F32, A_MISC + 2 * KB + 512)
    thf = ctx.at("l0b_thf", [128, 2048], F32, A_QFM)
    scrf = ctx.at("l0b_scr", [128, 2048], F32, A_QFM + 8 * KB)
    kff = ctx.at("l0b_kf", [128, 2048], F32, A_QFM + 16 * KB)
    iscr = ctx.at("l0b_iscr", [128, 2048], I32, A_QFM + 24 * KB)
    kstage = ctx.at("l0b_kstage", [128, 4096], BF16, A_KH)
    vstg = ctx.at("l0b_vstg", [128, 2048], BF16, A_VSTG)
    qfm = ctx.at("l0b_qfm", [128, 4096], BF16, P_AX)

    emit_weight_load(ctx, wq, dr["l0b_WqT"], 8, 128)
    emit_weight_load(ctx, wk, dr["l0b_WkT"], 8, 128)
    emit_weight_load(ctx, wv, dr["l0b_WvT"], 8, 128)
    emit_weight_load(ctx, wo, dr["l0b_WoT"], 1, 1024)
    emit_row_load(ctx, cols[:, 0:1], dr["l0b_wav"][:, :])
    emit_row_load(ctx, cols[:, 1:2], dr["l0b_phase"][:, :])
    emit_row_load(ctx, cols[:, 2:3], dr["l0b_bq"][:, :])
    emit_row_load(ctx, cols[:, 3:4], dr["l0b_bk"][:, :])
    emit_row_load(ctx, bvr[0:1, :], dr["l0b_bv"][0:1, :])
    wld0 = emit_row_load(ctx, bor[0:1, :], dr["l0b_bo"][0:1, :])

    @blk.vector
    def _(v):
        ctx.wait(v, "wld", wld0)

        def spacer():
            v.tensor_copy(colscr[:, :], colscr[:, :])
        v.memset(colscr[:, :], 0.0)
        v.tensor_scalar_add(cols[:, 4:5], cols[:, 0:1], 1e-8)
        spacer()
        v.reciprocal(cols[:, 4:5], cols[:, 4:5])
        spacer()
        v.tensor_tensor(cols[:, 5:6], cols[:, 2:3], cols[:, 4:5], ALU.mult)
        v.tensor_tensor(cols[:, 6:7], cols[:, 3:4], cols[:, 4:5], ALU.mult)
        spacer()
        v.tensor_tensor(cols[:, 5:6], cols[:, 5:6], cols[:, 1:2], ALU.add)
        ins = v.tensor_tensor(cols[:, 6:7], cols[:, 6:7], cols[:, 1:2], ALU.add)
        ctx.inc("ve", ins)
    cols_done = ("ve", ctx.val("ve"))

    def proj_theta_trig(w_sb, thb_col, lnd, half, dst_stage):
        @blk.tensor
        def _(t):
            ctx.wait(t, "wld", wld0)
            ctx.wait(t, *lnd)
            ctx.wait(t, *cols_done)
            for tt in range(NT):
                ctx.wait_bank_free(t, tt)
            ins = None
            for fi in range(NF):
                for tt in range(NT):
                    ins = t.matmul(ctx.ps[tt][:, :], w_sb[:, fi * 128:(fi + 1) * 128],
                                   xln[:, fi * TL + tt * 512:fi * TL + (tt + 1) * 512],
                                   start=(fi == 0), stop=(fi == NF - 1))
            cnt = ctx.inc("pe", ins)
            for tt in range(NT):
                ctx.set_bank_ready(tt, "pe", cnt)

        @blk.scalar
        def _(s):
            ins = None
            for tt in range(NT):
                ctx.wait_bank_ready(s, tt)
                ins = s.activation(thf[:, tt * 512:(tt + 1) * 512],
                                   ctx.ps[tt][:, :], AF.Identity,
                                   scale=cols[:, 4:5], bias=thb_col)
            cnt = ctx.inc("se", ins)
            for tt in range(NT):
                ctx.release_bank(tt, "se", cnt)

        stg = ("st", ctx.val("st"))
        emit_trig_from_theta(ctx, dst_stage, 0, thf, scrf, kff, iscr, HALF_PI,
                             gate=stg)
        emit_trig_from_theta(ctx, dst_stage, 2048, thf, scrf, kff, iscr, 0.0,
                             gate=stg)

    for phase, half in ((0, (core ^ 1) % 2), (1, core % 2)):
        if phase == 0:
            emit_load_x(ctx, x_sb, st["xp_dram"])
            emit_ln(ctx, st, src_wait=("ld", ctx.val("ld")))
        else:
            gate = ("pe", ctx.val("pe"))
            emit_load_x2(ctx, x_sb, st["x_dram"], gate=gate)
            emit_ln(ctx, st, src_wait=[("ld", ctx.val("ld")), gate,
                                       ("st", ctx.val("st"))])
        lnd = ("ve", ctx.val("ve"))
        stg_gate = ("st", ctx.val("st"))

        proj_theta_trig(wk, cols[:, 6:7], lnd, half, kstage)

        @blk.sync
        def _(sy, half=half):
            ctx.wait(sy, "se")
            if stg_gate[1]:
                ctx.wait(sy, *stg_gate)
            for ci in range(2):
                o = 0
                for (g0, g1) in core_token_slices_half(half):
                    n = g1 - g0
                    ins = sy.dma_start(out=k_dram[ci * 128:(ci + 1) * 128, g0:g1],
                                       in_=kstage[:, ci * 2048 + o:ci * 2048 + o + n])
                    ctx.inc("st", ins, 16)
                    o += n

        emit_vtok_to_dram(ctx, st, wv, wld0, 128, v_dram, core, vstg,
                          bias_row=None, src_ready=lnd, consts=consts, half=half,
                          stg_gate=stg_gate)

    kv_done_st = ctx.val("st")

    proj_theta_trig(wq, cols[:, 5:6], ("ve", ctx.val("ve")), core % 2, qfm)
    q_done = ("se", ctx.val("se"))

    khb = ctx.at("l0b_khb", [128, 8192], BF16, A_QFM)
    vhb = ctx.at("l0b_vhb", [128, 4096], BF16, A_QFM + 16 * KB)
    ptile = ctx.at("l0b_ptile", [128, 2048], BF16, A_PT)
    fsb = ctx.at("l0b_fsb", [128, 512], BF16, A_FSB)
    srow = ctx.at("l0b_srow", [1, 1024], F32, A_XTR)
    srow_bf = ctx.at("l0b_srow_bf", [1, 1024], BF16, A_SRBF)
    attnout = ctx.at("l0b_attnout", [128, 2048], BF16, OFF_XLN)

    @blk.sync
    def _(sy):
        ctx.wait(sy, "st", kv_done_st)
        ctx.wait(sy, *q_done)
        for ci in range(2):
            ins = sy.dma_start(out=khb[:, ci * 4096:(ci + 1) * 4096],
                               in_=k_dram[ci * 128:(ci + 1) * 128, :])
            ctx.inc("kld", ins, 16)
        ins = sy.dma_start(
            out=vhb[:, :].rearrange("p (s c) -> p s c", c=128),
            in_=v_dram[:, :].rearrange("(s p) c -> p s c", p=128))
        ctx.inc("kld", ins, 16)
    kld_now = ctx.val("kld")

    sc = 128.0 ** -0.5
    for qt in range(NT):
        bO = 4 + (qt % 2)
        bS = 6
        for kg in range(8):
            @blk.tensor
            def _(t, qt=qt, kg=kg, bO=bO):
                ctx.wait(t, "kld", kld_now)
                if kg == 0:
                    ctx.wait_bank_free(t, bO)
                    ctx.wait_bank_free(t, bS)
                for ks in range(4):
                    ctx.wait_bank_free(t, ks)
                ins = None
                for ks in range(4):
                    kcol = kg * 512 + ks * 128
                    for ci in range(2):
                        ins = t.matmul(ctx.psA[:, ks * 512:(ks + 1) * 512],
                                       khb[:, ci * 4096 + kcol:ci * 4096 + kcol + 128],
                                       qfm[:, ci * 2048 + qt * 512:ci * 2048 + (qt + 1) * 512],
                                       start=(ci == 0), stop=(ci == 1))
                cnt = ctx.inc("pe", ins)
                for ks in range(4):
                    ctx.set_bank_ready(ks, "pe", cnt)

            @blk.scalar
            def _(s, qt=qt, kg=kg):
                ctx.wait_bank_ready(s, 0)
                ins = s.activation(ptile[:, :], ctx.psA[:, :], AF.Exp, scale=sc)
                cnt = ctx.inc("se", ins)
                for ks in range(4):
                    ctx.release_bank(ks, "se", cnt)
            pt_dep = ("se", ctx.val("se"))

            @blk.tensor
            def _(t, qt=qt, kg=kg, bO=bO, pt_dep=pt_dep):
                ctx.wait(t, *pt_dep)
                ins = None
                for ks in range(4):
                    ksub = kg * 4 + ks
                    t.matmul(ctx.ps[bO][:, :], vhb[:, ksub * 128:(ksub + 1) * 128],
                             ptile[:, ks * 512:(ks + 1) * 512],
                             start=(kg == 0 and ks == 0), stop=False)
                    ins = t.matmul(ctx.ps[bS][0:1, :], consts["ones_col"][:, :],
                                   ptile[:, ks * 512:(ks + 1) * 512],
                                   start=(kg == 0 and ks == 0),
                                   stop=(kg == 7 and ks == 3))
                cnt = ctx.inc("pe", ins)
                ctx.set_bank_ready(bO, "pe", cnt)
                ctx.set_bank_ready(bS, "pe", cnt)

        @blk.vector
        def _(v, qt=qt, bO=bO):
            ctx.wait_bank_ready(v, bS)
            v.tensor_copy(srow[0:1, 0:512], ctx.ps[bS][0:1, 0:512])
            v.tensor_copy(srow_bf[0:1, 0:512], srow[0:1, 0:512])
            v.reciprocal(srow[0:1, 512:1024], srow[0:1, 0:512])
            ins = v.tensor_copy(srow_bf[0:1, 512:1024], srow[0:1, 512:1024])
            cnt = ctx.inc("ve", ins)
            ctx.release_bank(bS, "ve", cnt)
        ve_srow = ctx.val("ve")

        @blk.tensor
        def _(t, qt=qt, bO=bO):
            ctx.wait(t, "ve", ve_srow)
            t.matmul(ctx.ps[bO][:, :], bvr[0:1, 0:128], srow_bf[0:1, 0:512],
                     start=False, stop=True)
            ctx.wait_bank_free(t, 7)
            ins = t.matmul(ctx.ps[7][:, :], consts["ones_row128"][:, :],
                           srow_bf[0:1, 512:1024], start=True, stop=True)
            cnt = ctx.inc("pe", ins)
            ctx.set_bank_ready(bO, "pe", cnt)
            ctx.set_bank_ready(7, "pe", cnt)
        fin_dep = ("pe", ctx.val("pe"))

        @blk.vector
        def _(v, qt=qt, bO=bO, fin_dep=fin_dep):
            ctx.wait(v, *fin_dep)
            v.tensor_copy(fsb[:, :], ctx.ps[7][:, :])
            ins = v.tensor_tensor(attnout[:, qt * 512:(qt + 1) * 512],
                                  ctx.ps[bO][:, :], fsb[:, :], ALU.mult)
            cnt = ctx.inc("ve", ins)
            ctx.release_bank(bO, "ve", cnt)
            ctx.release_bank(7, "ve", cnt)

    att_done = ("ve", ctx.val("ve"))

    emit_proj(ctx, attnout, 0, 1, wo, 1024, list(range(8)), TL,
              evac_resid(ctx, x_sb), bias_row=bor, consts=consts,
              wld_target=wld0, src_ready=att_done,
              src_tile=lambda fi, tt: attnout[:, tt * 512:(tt + 1) * 512])



def weights_l3_attn_b(inp):
    def perhead_T(w):
        w = np.asarray(w, dtype=np.float32)
        return np.ascontiguousarray(w.transpose(0, 2, 1).reshape(1024, 64))
    return {
        "l3b_Wd": _np(inp["L3_attn_b_W_down"]),
        "l3b_tk": perhead_T(inp["L3_attn_b_tokeys"]),
        "l3b_tq": perhead_T(inp["L3_attn_b_toqueries"]),
        "l3b_tv": perhead_T(inp["L3_attn_b_tovalues"]),
        "l3b_Wu": _np(inp["L3_attn_b_W_unify"]),
        "l3b_bu": _row(inp["L3_attn_b_b_unify"]),
    }


L3B_STAGE = [99]


def emit_l3_attn_b(ctx, st, core):
    blk = ctx.block
    nc = ctx.nc
    dr = st["dram"]
    consts = st["consts"]
    x_sb, xln = st["x"], st["xln"]

    k_dram = nc.dram_tensor("l3b_K", [D, T], BF16).ap()
    v_dram = nc.dram_tensor("l3b_V", [T, D], BF16).ap()

    wd = ctx.at("l3b_wd", [128, 8192], BF16, A_WB0)
    tkT = ctx.at("l3b_tk", [128, 512], BF16, A_WB1)
    tqT = ctx.at("l3b_tq", [128, 512], BF16, A_WB1 + KB)
    tvT = ctx.at("l3b_tv", [128, 512], BF16, A_WB1 + 2 * KB)
    xd = ctx.at("l3b_xd", [128, 16384], BF16, A_QFM)
    kst = ctx.at("l3b_kst", [128, 4096], BF16, A_KH)
    vstg = ctx.at("l3b_vstg", [128, 2048], BF16, A_VSTG)
    qfm = ctx.at("l3b_qfm", [128, 16384], BF16, OFF_XLN)

    emit_weight_load(ctx, wd, dr["l3b_Wd"], 8, 1024)
    emit_weight_load(ctx, tkT, dr["l3b_tk"], 8, 64)
    emit_weight_load(ctx, tqT, dr["l3b_tq"], 8, 64)
    wld0 = emit_weight_load(ctx, tvT, dr["l3b_tv"], 8, 64)

    kst_free = [None, None]

    def head_K(dst_dram, tT, half, src_ready):
        for h in range(16):
            slot = h % 2
            pb = (h % 2) * 64
            ch = h // 2

            @blk.tensor
            def _(t, h=h, pb=pb, ch=ch):
                ctx.wait(t, "wld", wld0)
                ctx.wait(t, *src_ready)
                for tt in range(NT):
                    ctx.wait_bank_free(t, tt)
                ins = None
                for tt in range(NT):
                    ins = t.matmul(ctx.ps[tt][0:64, :],
                                   tT[pb:pb + 64, ch * 64:(ch + 1) * 64],
                                   xd[pb:pb + 64, ch * TL + tt * 512:ch * TL + (tt + 1) * 512],
                                   start=True, stop=True)
                cnt = ctx.inc("pe", ins)
                for tt in range(NT):
                    ctx.set_bank_ready(tt, "pe", cnt)

            @blk.scalar
            def _(s, h=h, slot=slot):
                if kst_free[slot] is not None:
                    ctx.wait(s, *kst_free[slot])
                ins = None
                for tt in range(NT):
                    ctx.wait_bank_ready(s, tt)
                    ins = s.activation(kst[0:64, slot * 2048 + tt * 512:
                                           slot * 2048 + (tt + 1) * 512],
                                       ctx.ps[tt][0:64, :], AF.Copy)
                cnt = ctx.inc("se", ins)
                for tt in range(NT):
                    ctx.release_bank(tt, "se", cnt)

            @blk.sync
            def _(sy, h=h, slot=slot, half=half):
                ctx.wait(sy, "se")
                o = 0
                cnt = None
                for (g0, g1) in core_token_slices_half(half):
                    n = g1 - g0
                    ins = sy.dma_start(out=dst_dram[h * 64:(h + 1) * 64, g0:g1],
                                       in_=kst[0:64, slot * 2048 + o:slot * 2048 + o + n])
                    cnt = ctx.inc("st", ins, 16)
                    o += n
                kst_free[slot] = ("st", cnt)

    def head_K_qfm(tT, src_ready):
        for h in range(16):
            pb = (h % 2) * 64
            ch = h // 2

            @blk.tensor
            def _(t, h=h, pb=pb, ch=ch):
                ctx.wait(t, "wld", wld0)
                ctx.wait(t, *src_ready)
                for tt in range(NT):
                    ctx.wait_bank_free(t, tt)
                ins = None
                for tt in range(NT):
                    ins = t.matmul(ctx.ps[tt][0:64, :],
                                   tT[pb:pb + 64, ch * 64:(ch + 1) * 64],
                                   xd[pb:pb + 64, ch * TL + tt * 512:ch * TL + (tt + 1) * 512],
                                   start=True, stop=True)
                cnt = ctx.inc("pe", ins)
                for tt in range(NT):
                    ctx.set_bank_ready(tt, "pe", cnt)

            @blk.scalar
            def _(s, h=h, pb=pb, ch=ch):
                ins = None
                for tt in range(NT):
                    ctx.wait_bank_ready(s, tt)
                    ins = s.activation(qfm[pb:pb + 64, ch * TL + tt * 512:
                                           ch * TL + (tt + 1) * 512],
                                       ctx.ps[tt][0:64, :], AF.Copy)
                cnt = ctx.inc("se", ins)
                for tt in range(NT):
                    ctx.release_bank(tt, "se", cnt)

    def head_V(half, src_ready, stg_gate):
        vstg_free = [stg_gate, stg_gate]
        for tsub in range(TL // 128):
            slot = tsub % 2

            @blk.tensor
            def _(t, tsub=tsub):
                ctx.wait(t, "wld", wld0)
                ctx.wait(t, *src_ready)
                for g in range(2):
                    ctx.wait_bank_free(t, 6 + g)
                ins = None
                import os as _os5
                for h in range(16):
                    pb = (h % 2) * 64
                    ch = h // 2
                    g = h % 2
                    off = (h // 2) * 64
                    ins = t.matmul(ctx.ps[6 + g][:, off:off + 64],
                                   xd[pb:pb + 64, ch * TL + tsub * 128:
                                      ch * TL + (tsub + 1) * 128],
                                   tvT[pb:pb + 64, ch * 64:(ch + 1) * 64],
                                   start=True, stop=True)
                cnt = ctx.inc("pe", ins)
                for g in range(2):
                    ctx.set_bank_ready(6 + g, "pe", cnt)

            @blk.vector
            def _(v, tsub=tsub, slot=slot):
                if vstg_free[slot] is not None:
                    ctx.wait(v, *vstg_free[slot])
                ctx.wait_bank_ready(v, 6)
                ins = None
                for g in range(2):
                    ins = v.tensor_copy(vstg[:, slot * 1024 + g * 512:
                                             slot * 1024 + (g + 1) * 512],
                                        ctx.ps[6 + g][:, :])
                cnt = ctx.inc("ve", ins)
                for g in range(2):
                    ctx.release_bank(6 + g, "ve", cnt)

            @blk.sync
            def _(sy, tsub=tsub, slot=slot, half=half):
                ctx.wait(sy, "ve")
                g = glob_of(half, tsub * 128)
                ins = sy.dma_start(out=v_dram[g:g + 128, 0:D],
                                   in_=vstg[:, slot * 1024:(slot + 1) * 1024])
                cnt = ctx.inc("st", ins, 16)
                vstg_free[slot] = ("st", cnt)

    for phase, half in ((0, (core ^ 1) % 2), (1, core % 2)):
        if phase == 0:
            emit_load_x(ctx, x_sb, st["xp_dram"])
            emit_ln(ctx, st, src_wait=("ld", ctx.val("ld")))
        else:
            gate = ("pe", ctx.val("pe"))
            emit_load_x2(ctx, x_sb, st["x_dram"], gate=gate)
            emit_ln(ctx, st, src_wait=[("ld", ctx.val("ld")), gate,
                                       ("st", ctx.val("st"))])
        lnd = ("ve", ctx.val("ve"))

        emit_proj(ctx, xln, 0, NF, wd, 1024, list(range(8)), TL,
                  evac_simple(ctx, xd, 0, TL, eng="vector"),
                  consts=consts, wld_target=wld0, src_ready=lnd)
        xd_done = ("ve", ctx.val("ve"))

        if L3B_STAGE[0] >= 2:
            head_K(k_dram, tkT, half, xd_done)
        if L3B_STAGE[0] >= 3:
            head_V(half, xd_done, ("st", ctx.val("st")))
        if phase == 1 and L3B_STAGE[0] >= 5:
            head_K_qfm(tqT, xd_done)
        if L3B_STAGE[0] <= 3 and phase == 0:
            return

    kv_done_st = ctx.val("st")
    q_done = ("se", ctx.val("se"))
    if L3B_STAGE[0] <= 5:
        return

    attnout = ctx.at("l3b_attnout", [128, 16384], BF16, A_QFM)
    cfg = {
        "H": 16, "dh": 64, "k_dram": k_dram, "v_dram": v_dram,
        "exp_scale": 0.125, "norm": True, "v_bias_row": None,
        "out_sb": attnout, "qfm": qfm, "kv_ready": kv_done_st,
        "q_slice": lambda h, qt: qfm[(h % 2) * 64:(h % 2) * 64 + 64,
                                     (h // 2) * TL + qt * 512:(h // 2) * TL + (qt + 1) * 512],
        "out_base": lambda h: (h // 2, (h % 2) * 64),
        "q_ready": q_done,
        "kld_gate": q_done,
        "v_col": lambda h: (h % 2) * 512 + (h // 2) * 64,
    }
    emit_attn_core(ctx, st, cfg)
    att_done = ("ve", ctx.val("ve"))

    wu = ctx.at("l3b_wu", [128, 8192], BF16, A_WB0)
    bur = ctx.at("l3b_bu", [1, 1024], BF16, A_MISC)
    emit_row_load(ctx, bur[0:1, :], dr["l3b_bu"][0:1, :], gate=att_done)
    wldu = emit_weight_load(ctx, wu, dr["l3b_Wu"], 8, 1024,
                            gate=("pe", ctx.val("pe")))
    emit_proj(ctx, attnout, 0, NF, wu, 1024, list(range(8)), TL,
              evac_resid(ctx, x_sb), bias_row=bur, consts=consts,
              wld_target=wldu, src_ready=att_done)



def weights_l0_attn_a(inp):
    def wall(w):
        w = np.asarray(w, dtype=np.float32)
        return np.ascontiguousarray(w.transpose(1, 0, 2).reshape(1024, 1024))

    def aall(a):
        a = np.asarray(a, dtype=np.float32)
        return np.ascontiguousarray(a.transpose(1, 0, 2).reshape(1024, 128))

    def bblk(b):
        b = np.asarray(b, dtype=np.float32)
        out = np.zeros((128, 1024), dtype=np.float32)
        for h in range(16):
            out[h * 8:(h + 1) * 8, h * 64:(h + 1) * 64] = b[h]
        return out

    return {
        "l0a_Wq": wall(inp["L0_attn_a_Wq"]),
        "l0a_Wk": wall(inp["L0_attn_a_Wk"]),
        "l0a_Wv": wall(inp["L0_attn_a_Wv"]),
        "l0a_Aq": aall(inp["L0_attn_a_Aq"]),
        "l0a_Ak": aall(inp["L0_attn_a_Ak"]),
        "l0a_Av": aall(inp["L0_attn_a_Av"]),
        "l0a_Bq": bblk(inp["L0_attn_a_Bq"]),
        "l0a_Bk": bblk(inp["L0_attn_a_Bk"]),
        "l0a_Bv": bblk(inp["L0_attn_a_Bv"]),
        "l0a_Wp": _np(inp["L0_attn_a_Wp"]),
        "l0a_Ap": _np(inp["L0_attn_a_Ap"]),
        "l0a_Bp": _np(inp["L0_attn_a_Bp"]),
        "l0a_bp": _row(inp["L0_attn_a_bp"]),
    }


def causal_band(core, anti=False, nband=8, bias=-10000.0):
    p = core % 2
    out = np.zeros((128, nband * 512), dtype=np.float32)
    kk = np.arange(128)[:, None]
    qq = np.arange(512)[None, :]
    for s in range(nband):
        if s < 4:
            diag = (kk + s * 128) > qq if anti else (kk + s * 128) <= qq
            keep = diag
        else:
            above = (p == 0)
            keep = np.full((128, 512), above if anti else not above)
        out[:, s * 512:(s + 1) * 512] = np.where(keep, 0.0, bias)
    return out


def l0a_qt_glob(core):
    p = core % 2
    return [(2 * i + p) * 512 for i in range(4)]


L0A_STAGE = [99]


def emit_l0_attn_a(ctx, st, core):
    blk = ctx.block
    nc = ctx.nc
    dr = st["dram"]
    consts = st["consts"]
    x_sb, xln = st["x"], st["xln"]

    k_dram = nc.dram_tensor("l0a_K", [D, T], BF16).ap()
    v_dram = nc.dram_tensor("l0a_V", [T, D], BF16).ap()

    wk = ctx.at("l0a_wk", [128, 8192], BF16, A_WB0)
    wv = ctx.at("l0a_wv", [128, 8192], BF16, A_WB1)
    stage = ctx.at("l0a_stage", [128, 16384], BF16, A_QFM)
    vstg = ctx.at("l0a_vstg", [128, 2048], BF16, A_VSTG)
    a_sb = ctx.at("l0a_aall", [128, 3072], BF16, A_KH)
    zq = ctx.at("l0a_zq", [128, 2048], BF16, P_AX)
    zk = ctx.at("l0a_zk", [128, 2048], BF16, P_AX + 4 * KB)
    zv = ctx.at("l0a_zv", [128, 2048], BF16, P_AX + 8 * KB)
    bslot = ctx.at("l0a_bslot", [128, 1024], BF16, P_AX + 12 * KB)
    band = ctx.at("l0a_band", [128, 4096], BF16, A_WB0)
    bpr = ctx.at("l0a_bp", [1, 1024], BF16, A_MISC)

    emit_weight_load(ctx, wk, dr["l0a_Wk"], 8, 1024)
    emit_weight_load(ctx, wv, dr["l0a_Wv"], 8, 1024)

    @blk.gpsimd
    def _(g):
        for fi in range(NF):
            for j, nm in enumerate(("l0a_Aq", "l0a_Ak", "l0a_Av")):
                ins = g.dma_start(out=a_sb[:, fi * 384 + j * 128:fi * 384 + (j + 1) * 128],
                                  in_=dr[nm][fi * 128:(fi + 1) * 128, :])
                ctx.inc("wld", ins, 16)
        ins = g.dma_start(out=bpr[0:1, :], in_=dr["l0a_bp"][0:1, :])
        ctx.inc("wld", ins, 16)
    wld0 = ctx.val("wld")

    def load_bslot(nm, gate):
        return emit_row_load(ctx, bslot[:, :], dr[nm][:, :], gate=gate)

    def zdst(fo):
        return (zq, zk, zv)[fo]

    def zcb(fo, banks):
        @blk.vector
        def _(v, fo=fo, banks=banks):
            ins = None
            dst = zdst(fo)
            for tt in range(NT):
                ctx.wait_bank_ready(v, banks[tt])
                ins = v.tensor_scalar(dst[:, tt * 512:(tt + 1) * 512],
                                      ctx.ps[banks[tt]][:, :], 2.0, None, ALU.mult)
            cnt = ctx.inc("ve", ins)
            for tt in range(NT):
                ctx.release_bank(banks[tt], "ve", cnt)

    for phase, half in ((0, (core ^ 1) % 2), (1, core % 2)):
        if phase == 0:
            emit_load_x(ctx, x_sb, st["xp_dram"])
            emit_ln(ctx, st, src_wait=("ld", ctx.val("ld")))
        else:
            gate = ("pe", ctx.val("pe"))
            emit_load_x2(ctx, x_sb, st["x_dram"], gate=gate)
            emit_ln(ctx, st, src_wait=[("ld", ctx.val("ld")), gate,
                                       ("st", ctx.val("st"))])
        lnd = ("ve", ctx.val("ve"))
        stg_gate = ("st", ctx.val("st"))

        emit_proj(ctx, xln, 0, NF, a_sb, 384, [0, 1, 2], TL, zcb,
                  consts=consts, wld_target=wld0, src_ready=lnd)
        z_done = ("se", ctx.val("se"))

        wldk = load_bslot("l0a_Bk", ("pe", ctx.val("pe")))
        emit_kfm_to_dram(ctx, st, wk, max(wld0, wldk), NF, k_dram, core, stage,
                         src_ready=lnd, consts=consts, half=half,
                         stage_gate=stg_gate,
                         extra=[(lambda fo: bslot[:, fo * 128:(fo + 1) * 128],
                                 lambda tt: zk[:, tt * 512:(tt + 1) * 512])],
                         extra_ready=z_done)

        wldv = load_bslot("l0a_Bv", ("pe", ctx.val("pe")))
        emit_vtok_to_dram(ctx, st, wv, max(wld0, wldv), D, v_dram, core, vstg,
                          src_ready=[lnd, z_done], consts=consts, half=half,
                          stg_gate=stg_gate,
                          extra=(lambda tsub: zv[:, tsub * 128:(tsub + 1) * 128],
                                 lambda g, nn: bslot[:, g * 512:g * 512 + nn]))

    kv_done_st = ctx.val("st")
    p2_pe = ("pe", ctx.val("pe"))

    wq = ctx.at("l0a_wq", [128, 8192], BF16, A_WB0)
    wldq = emit_weight_load(ctx, wq, dr["l0a_Wq"], 8, 1024, gate=p2_pe)
    wldbq = load_bslot("l0a_Bq", p2_pe)
    qfm = ctx.at("l0a_qfm", [128, 16384], BF16, A_QFM)
    emit_proj(ctx, xln, 0, NF, wq, 1024, list(range(8)), TL,
              evac_simple(ctx, qfm, 0, TL, eng="vector"),
              consts=consts, wld_target=max(wldq, wldbq),
              src_ready=("st", kv_done_st),
              extra=[(lambda fo: bslot[:, fo * 128:(fo + 1) * 128],
                      lambda tt: zq[:, tt * 512:(tt + 1) * 512])])
    q_done = ("ve", ctx.val("ve"))
    wldband = emit_row_load(ctx, band[:, :], dr["l0a_band"][:, :], gate=q_done)

    attnout = ctx.at("l0a_attnout", [128, 16384], BF16, OFF_XLN)
    qtg = l0a_qt_glob(core)

    if L0A_STAGE[0] <= 5:
        if DEBUG_TAPS:
            nc3 = ctx.nc
            dbg_z2 = nc3.dram_tensor("dbg_z2", [128, 3 * TL], F32,
                                     kind="ExternalOutput").ap()
            dbg_a = nc3.dram_tensor("dbg_a", [128, 3072], F32,
                                    kind="ExternalOutput").ap()
            dbg_q2 = nc3.dram_tensor("dbg_q2", [D, TL], F32,
                                     kind="ExternalOutput").ap()

            @blk.gpsimd
            def _(g):
                ctx.wait(g, "ve")
                ctx.wait(g, "se")
                ins = g.dma_start(out=dbg_a[:, :], in_=a_sb[:, :])
                ctx.inc("st", ins, 16)
                for i, zz in enumerate((zq, zk, zv)):
                    ins = g.dma_start(out=dbg_z2[:, i * TL:(i + 1) * TL], in_=zz[:, :])
                    ctx.inc("st", ins, 16)
                for c in range(NF):
                    ins = g.dma_start(out=dbg_q2[c * 128:(c + 1) * 128, :],
                                      in_=qfm[:, c * TL:(c + 1) * TL])
                    ctx.inc("st", ins, 16)
        return

    def kspec(qt, ksub):
        s = ksub - 8 * qt
        if s < 0:
            return "full"
        if s >= 8:
            return "skip"
        return ("bias", band[:, s * 512:(s + 1) * 512])

    cfg = {
        "H": 16, "dh": 64, "k_dram": k_dram, "v_dram": v_dram,
        "exp_scale": 0.125, "norm": True, "v_bias_row": None,
        "out_sb": attnout, "qfm": qfm, "kv_ready": kv_done_st,
        "q_slice": lambda h, qt: qfm[(h % 2) * 64:(h % 2) * 64 + 64,
                                     (h // 2) * TL + qt * 512:(h // 2) * TL + (qt + 1) * 512],
        "out_base": lambda h: (h // 2, (h % 2) * 64),
        "q_ready": q_done,
        "kld_gate": q_done,
        "pre_gate": ("pe", ctx.val("pe")),
        "ksub_spec": kspec,
        "kld_gate2": ("wld", wldband),
    }
    emit_attn_core(ctx, st, cfg)
    att_done = ("ve", ctx.val("ve"))

    if DEBUG_TAPS:
        nc2 = ctx.nc
        dbg_k = nc2.dram_tensor("dbg_k", [D, T], F32, kind="ExternalOutput").ap()
        dbg_v = nc2.dram_tensor("dbg_v", [T, D], F32, kind="ExternalOutput").ap()
        dbg_q = nc2.dram_tensor("dbg_q", [D, TL], F32, kind="ExternalOutput").ap()
        dbg_att = nc2.dram_tensor("dbg_att", [D, TL], F32, kind="ExternalOutput").ap()

        dbg_z = nc2.dram_tensor("dbg_z", [128, 3 * TL], F32, kind="ExternalOutput").ap()

        @blk.gpsimd
        def _(g):
            ctx.wait(g, "ve")
            ins = g.dma_start(out=dbg_k[:, :], in_=k_dram[:, :])
            ctx.inc("st", ins, 16)
            ins = g.dma_start(out=dbg_v[:, :], in_=v_dram[:, :])
            ctx.inc("st", ins, 16)
            for i, zz in enumerate((zq, zk, zv)):
                ins = g.dma_start(out=dbg_z[:, i * TL:(i + 1) * TL], in_=zz[:, :])
                ctx.inc("st", ins, 16)
            for c in range(NF):
                ins = g.dma_start(out=dbg_q[c * 128:(c + 1) * 128, :],
                                  in_=qfm[:, c * TL:(c + 1) * TL])
                ctx.inc("st", ins, 16)
                ins = g.dma_start(out=dbg_att[c * 128:(c + 1) * 128, :],
                                  in_=attnout[:, c * TL:(c + 1) * TL])
                ctx.inc("st", ins, 16)

    wp = ctx.at("l0a_wp", [128, 8192], BF16, A_WB1)
    ap_sb = ctx.at("l0a_ap", [128, 64], BF16, A_VH)
    bp_sb = ctx.at("l0a_bpmat", [8, 1024], BF16, A_VH + KB)
    zp = ctx.at("l0a_zp", [8, 2048], BF16, A_VH + 4 * KB)

    @blk.gpsimd
    def _(g):
        ctx.wait(g, *att_done)
        for fi in range(NF):
            ins = g.dma_start(out=ap_sb[:, fi * 8:(fi + 1) * 8],
                              in_=dr["l0a_Ap"][fi * 128:(fi + 1) * 128, :])
            ctx.inc("wld", ins, 16)
        ins = g.dma_start(out=bp_sb[0:8, :], in_=dr["l0a_Bp"][0:8, :])
        ctx.inc("wld", ins, 16)
    wldp2 = ctx.val("wld")
    wldp = emit_weight_load(ctx, wp, dr["l0a_Wp"], 8, 1024, gate=p2_pe)

    @blk.tensor
    def _(t):
        ctx.wait(t, "wld", wldp2)
        ctx.wait(t, *att_done)
        for tt in range(NT):
            ctx.wait_bank_free(t, tt)
        ins = None
        for fi in range(NF):
            for tt in range(NT):
                ins = t.matmul(ctx.ps[tt][0:8, :], ap_sb[:, fi * 8:(fi + 1) * 8],
                               attnout[:, fi * TL + tt * 512:fi * TL + (tt + 1) * 512],
                               start=(fi == 0), stop=(fi == NF - 1))
        cnt = ctx.inc("pe", ins)
        for tt in range(NT):
            ctx.set_bank_ready(tt, "pe", cnt)

    @blk.scalar
    def _(s):
        ins = None
        for tt in range(NT):
            ctx.wait_bank_ready(s, tt)
            ins = s.activation(zp[0:8, tt * 512:(tt + 1) * 512],
                               ctx.ps[tt][0:8, :], AF.Identity, scale=2.0)
        cnt = ctx.inc("se", ins)
        for tt in range(NT):
            ctx.release_bank(tt, "se", cnt)
    zp_done = ("se", ctx.val("se"))

    emit_proj(ctx, attnout, 0, NF, wp, 1024, list(range(8)), TL,
              evac_resid(ctx, x_sb), bias_row=bpr, consts=consts,
              wld_target=wldp, src_ready=zp_done,
              extra=[(lambda fo: bp_sb[0:8, fo * 128:(fo + 1) * 128],
                      lambda tt: zp[0:8, tt * 512:(tt + 1) * 512])])



def weights_l1_attn_a(inp):
    qkv = _np(inp["L1_attn_a_qkv_w"])
    qkvb = np.asarray(inp["L1_attn_a_qkv_b"], dtype=np.float32)
    return {
        "l1a_wq": np.ascontiguousarray(qkv[:, 0:D]),
        "l1a_wk": np.ascontiguousarray(qkv[:, D:2 * D]),
        "l1a_wv": np.ascontiguousarray(qkv[:, 2 * D:3 * D]),
        "l1a_bq": np.ascontiguousarray(qkvb[0:D].reshape(1, D)),
        "l1a_bk": np.ascontiguousarray(qkvb[D:2 * D].reshape(1, D)),
        "l1a_bv": np.ascontiguousarray(qkvb[2 * D:3 * D].reshape(1, D)),
        "l1a_projw": _np(inp["L1_attn_a_proj_w"]),
        "l1a_projb": _row(inp["L1_attn_a_proj_b"]),
        "l1a_g1": _col128(inp["L1_attn_a_g1"]),
        "l1a_b1": _col128(inp["L1_attn_a_b1"]),
        "l1a_g2": _col128(inp["L1_attn_a_g2"]),
        "l1a_b2": _col128(inp["L1_attn_a_b2"]),
        "l1a_fc1": _np(inp["L1_attn_a_fc1_w"]),
        "l1a_fb1": _col128(inp["L1_attn_a_fc1_b"]),
        "l1a_fc2": _np(inp["L1_attn_a_fc2_w"]),
        "l1a_fb2": _col128(inp["L1_attn_a_fc2_b"]),
    }


def l1a_percore(core):
    p = core % 2
    selA = np.full((8, 1024), 1.0 - p, dtype=np.float32)
    selB = np.full((8, 1024), float(p), dtype=np.float32)
    bm = np.ones((128, 192), dtype=np.float32)
    if p == 0:
        bm[0, 16:48] = 0.0
        bm[1, 32:48] = 0.0
    return {"l1a_selA": selA, "l1a_selB": selB, "l1a_bmask": bm}


L1A_STAGE = [99]


def emit_l1_attn_a(ctx, st, core):
    blk = ctx.block
    nc = ctx.nc
    dr = st["dram"]
    consts = st["consts"]
    x_sb, xln = st["x"], st["xln"]
    y_dram = st["y_dram"]

    q_dram = nc.dram_tensor("l1a_Qd", [T, D], BF16).ap()
    k_dram = nc.dram_tensor("l1a_Kd", [T, D], BF16).ap()
    v_dram = nc.dram_tensor("l1a_Vd", [T, D], BF16).ap()
    h0_dram = nc.dram_tensor("l1a_h0", [D, TL], BF16).ap()
    h_dram = nc.dram_tensor("l1a_hsp", [4 * D, TL], BF16).ap()
    bnd_dram = nc.dram_tensor("l1a_bnd", [8, 2048], BF16).ap()

    wslot = ctx.at("l1a_w", [128, 8192], BF16, A_WB0)
    vstg = ctx.at("l1a_vstg", [128, 2048], BF16, A_VSTG)
    bqr = ctx.at("l1a_bqr", [1, 1024], BF16, OFF_MISC)
    bkr = ctx.at("l1a_bkr", [1, 1024], BF16, OFF_MISC + 2048)
    bvr = ctx.at("l1a_bvr", [1, 1024], BF16, OFF_MISC + 4096)
    bpr = ctx.at("l1a_bpr", [1, 1024], BF16, OFF_MISC + 6144)
    g1c = ctx.at("l1a_g1c", [128, 8], F32, OFF_MISC + 8192)
    b1c = ctx.at("l1a_b1c", [128, 8], F32, OFF_MISC + 8224)
    g2c = ctx.at("l1a_g2c", [128, 8], F32, OFF_MISC + 8256)
    b2c = ctx.at("l1a_b2c", [128, 8], F32, OFF_MISC + 8288)
    fb2c = ctx.at("l1a_fb2c", [128, 8], F32, OFF_MISC + 8320)
    fb1c = ctx.at("l1a_fb1c", [128, 32], F32, OFF_MISC + 8352)
    mask_sb = ctx.at("l1a_mask", [128, 192], F32, OFF_MISC + 8480)

    TA = OFF_TAIL3
    q_t = ctx.at("l1a_qt", [128, 1024], BF16, TA)
    k_t = [ctx.at("l1a_kt%d" % i, [128, 1024], BF16, TA + (2 + 2 * i) * KB)
           for i in range(3)]
    v_t = [ctx.at("l1a_vt%d" % i, [128, 1024], BF16, TA + (8 + 2 * i) * KB)
           for i in range(3)]
    mbuf = ctx.at("l1a_m", [128, 1024], F32, TA + 14 * KB)
    obuf = ctx.at("l1a_o", [128, 1024], F32, TA + 18 * KB)
    obf = ctx.at("l1a_obf", [128, 1024], BF16, TA + 22 * KB)
    sp = ctx.at("l1a_sp", [128, 48], F32, TA + 24 * KB)
    pp = ctx.at("l1a_pp", [128, 48], F32, TA + 24 * KB + 256)
    den = ctx.at("l1a_den", [128, 16], F32, TA + 24 * KB + 512)
    rcl = ctx.at("l1a_rc", [128, 16], F32, TA + 24 * KB + 576)
    bnd_k = ctx.at("l1a_bndk", [8, 1024], BF16, TA + 25 * KB)
    bnd_v = ctx.at("l1a_bndv", [8, 1024], BF16, TA + 27 * KB)
    selA = ctx.at("l1a_selA", [8, 1024], BF16, TA + 29 * KB)
    selB = ctx.at("l1a_selB", [8, 1024], BF16, TA + 31 * KB)

    for tile, name in ((bqr, "l1a_bq"), (bkr, "l1a_bk"), (bvr, "l1a_bv"),
                       (bpr, "l1a_projb")):
        emit_row_load(ctx, tile[0:1, :], dr[name][0:1, :])
    for tile, name in ((g1c, "l1a_g1"), (b1c, "l1a_b1"), (g2c, "l1a_g2"),
                       (b2c, "l1a_b2"), (fb2c, "l1a_fb2"), (fb1c, "l1a_fb1"),
                       (mask_sb, "l1a_bmask")):
        emit_row_load(ctx, tile[:, :], dr[name][:, :])
    emit_row_load(ctx, selA[:, :], dr["l1a_selA"][:, :])
    emit_row_load(ctx, selB[:, :], dr["l1a_selB"][:, :])
    wld_rows = ctx.val("wld")

    ln1_aff = (lambda c: g1c[:, c:c + 1], lambda c: b1c[:, c:c + 1])

    for phase, hv in ((0, 1), (1, 0)):
        if phase == 0:
            emit_load_x(ctx, x_sb, st["xp_dram"])
            emit_ln(ctx, st, src_wait=("ld", ctx.val("ld")))
        else:
            gate = ("pe", ctx.val("pe"))
            emit_load_x2(ctx, x_sb, st["x_dram"], gate=gate)
            emit_ln(ctx, st, src_wait=[("ld", ctx.val("ld")), gate,
                                       ("st", ctx.val("st"))])
        ln_done = ("ve", ctx.val("ve"))
        ln1_wait = [ln_done, ("wld", wld_rows)]

        if phase == 1:
            @blk.gpsimd
            def _(g, ln_done=ln_done):
                ctx.wait(g, *ln_done)
                for c in range(NF):
                    ins = g.dma_start(out=h0_dram[c * 128:(c + 1) * 128, :],
                                      in_=xln[:, c * TL:(c + 1) * TL])
                    ctx.inc("st", ins, 16)
            ln1_wait.append(("st", ctx.val("st")))

            @blk.vector
            def _(v, ln_done=ln_done):
                ctx.wait(v, *ln_done)
                ins = None
                for c in range(NF):
                    ins = v.tensor_tensor(x_sb[:, c * TL:(c + 1) * TL],
                                          x_sb[:, c * TL:(c + 1) * TL],
                                          xln[:, c * TL:(c + 1) * TL], ALU.add)
                ctx.inc("ve", ins)

        emit_ln(ctx, st, src_wait=ln1_wait, src=xln, dst=xln, affine=ln1_aff,
                inplace=True)
        h1_done = ("se", ctx.val("se"))

        for wname, brow, dram_t in (("l1a_wq", bqr, q_dram),
                                    ("l1a_wk", bkr, k_dram),
                                    ("l1a_wv", bvr, v_dram)):
            wgate = ("pe", ctx.val("pe"))
            wldw = emit_weight_load(ctx, wslot, dr[wname], 8, 1024, gate=wgate)
            emit_vtok_to_dram(ctx, st, wslot, max(wldw, wld_rows), D, dram_t,
                              core, vstg, bias_row=brow, src_ready=[h1_done],
                              consts=consts, half=hv,
                              stg_gate=("st", ctx.val("st")), src_sb=xln)
    qkv_done_st = ctx.val("st")
    prod_pe = ("pe", ctx.val("pe"))

    @blk.sync
    def _(sy):
        ctx.wait(sy, "st", qkv_done_st)
        for b in range(4):
            rA = max(1024 * b - 2, 0)
            rB = 1024 * b + 1022
            ins = sy.dma_start(out=q_t[2 * b:2 * b + 2, :], in_=k_dram[rA:rA + 2, :])
            ctx.inc("bld", ins, 16)
            ins = sy.dma_start(out=k_t[0][2 * b:2 * b + 2, :],
                               in_=k_dram[rB:rB + 2, :])
            ctx.inc("bld", ins, 16)
            ins = sy.dma_start(out=k_t[1][2 * b:2 * b + 2, :],
                               in_=v_dram[rA:rA + 2, :])
            ctx.inc("bld", ins, 16)
            ins = sy.dma_start(out=k_t[2][2 * b:2 * b + 2, :],
                               in_=v_dram[rB:rB + 2, :])
            ctx.inc("bld", ins, 16)
    bld_now = ctx.val("bld")

    @blk.vector
    def _(v):
        ctx.wait(v, "bld", bld_now)
        ctx.wait(v, "wld", wld_rows)
        v.tensor_tensor(bnd_k[0:8, :], q_t[0:8, :], selA[0:8, :], ALU.mult)
        v.tensor_tensor(k_t[0][0:8, :], k_t[0][0:8, :], selB[0:8, :], ALU.mult)
        v.tensor_tensor(bnd_k[0:8, :], bnd_k[0:8, :], k_t[0][0:8, :], ALU.add)
        v.tensor_tensor(bnd_v[0:8, :], k_t[1][0:8, :], selA[0:8, :], ALU.mult)
        v.tensor_tensor(k_t[2][0:8, :], k_t[2][0:8, :], selB[0:8, :], ALU.mult)
        ins = v.tensor_tensor(bnd_v[0:8, :], bnd_v[0:8, :], k_t[2][0:8, :],
                              ALU.add)
        ctx.inc("ve", ins)
    bnd_ve = ctx.val("ve")

    @blk.gpsimd
    def _(g):
        ctx.wait(g, "ve", bnd_ve)
        ins = g.dma_start(out=bnd_dram[:, 0:1024], in_=bnd_k[0:8, :])
        ctx.inc("st", ins, 16)
        ins = g.dma_start(out=bnd_dram[:, 1024:2048], in_=bnd_v[0:8, :])
        ctx.inc("st", ins, 16)
    bnd_st = ctx.val("st")

    attnout = ctx.at("l1a_attnout", [128, 16384], BF16, OFF_LNS)
    tile_free = [("st", bnd_st)]
    obf_free = [None]

    for b in range(4):
        for j in range(4):
            row0 = 1024 * b + 128 * j

            @blk.sync
            def _(sy, b=b, j=j, row0=row0):
                ctx.wait(sy, "st", qkv_done_st)
                if tile_free[0] is not None:
                    ctx.wait(sy, *tile_free[0])
                ins = sy.dma_start(out=q_t[:, :], in_=q_dram[row0:row0 + 128, :])
                ctx.inc("tld", ins, 16)
                for dlt in range(3):
                    shl = [(k_t[dlt], k_dram, 0), (v_t[dlt], v_dram, 1024)]
                    for buf, src, bo in shl:
                        if dlt == 0:
                            ins = sy.dma_start(out=buf[:, :],
                                               in_=src[row0:row0 + 128, :])
                            ctx.inc("tld", ins, 16)
                        elif j == 0:
                            ins = sy.dma_start(
                                out=buf[dlt:128, :],
                                in_=src[row0:row0 + 128 - dlt, :])
                            ctx.inc("tld", ins, 16)
                            ins = sy.dma_start(
                                out=buf[0:dlt, :],
                                in_=bnd_dram[2 * b + 2 - dlt:2 * b + 2,
                                             bo:bo + 1024])
                            ctx.inc("tld", ins, 16)
                        else:
                            ins = sy.dma_start(
                                out=buf[:, :],
                                in_=src[row0 - dlt:row0 + 128 - dlt, :])
                            ctx.inc("tld", ins, 16)
            tld_now = ctx.val("tld")

            @blk.vector
            def _(v, b=b, j=j, tld_now=tld_now):
                ctx.wait(v, "tld", tld_now)
                for dlt in range(3):
                    v.tensor_tensor(mbuf[:, :], q_t[:, :], k_t[dlt][:, :],
                                    ALU.mult)
                    m3 = mbuf[:, :].rearrange("p (h d) -> p h d", d=64)
                    v.tensor_reduce(sp[:, dlt * 16:(dlt + 1) * 16], m3, AX.X,
                                    ALU.add)
                ins = v.tensor_copy(mbuf[:, 0:512], mbuf[:, 0:512])
                ctx.inc("ve", ins)
            ve_s = ctx.val("ve")

            @blk.scalar
            def _(s, ve_s=ve_s):
                ctx.wait(s, "ve", ve_s)
                ins = s.activation(pp[:, :], sp[:, :], AF.Exp, scale=0.125)
                ctx.inc("se", ins)
            se_p = ctx.val("se")

            @blk.vector
            def _(v, b=b, j=j, se_p=se_p):
                ctx.wait(v, "se", se_p)
                if obf_free[0] is not None:
                    ctx.wait(v, *obf_free[0])
                if j == 0:
                    v.tensor_tensor(pp[:, :], pp[:, :],
                                    mask_sb[:, b * 48:(b + 1) * 48], ALU.mult)
                    v.tensor_copy(mbuf[:, 0:512], mbuf[:, 0:512])
                v.tensor_tensor(den[:, :], pp[:, 0:16], pp[:, 16:32], ALU.add)
                v.tensor_copy(mbuf[:, 0:512], mbuf[:, 0:512])
                v.tensor_tensor(den[:, :], den[:, :], pp[:, 32:48], ALU.add)
                v.tensor_copy(mbuf[:, 512:1024], mbuf[:, 512:1024])
                v.reciprocal(rcl[:, :], den[:, :])
                v.tensor_copy(mbuf[:, 0:512], mbuf[:, 0:512])
                o3 = obuf[:, :].rearrange("p (h d) -> p h d", d=64)
                m3 = mbuf[:, :].rearrange("p (h d) -> p h d", d=64)
                for dlt in range(3):
                    p3 = pp[:, dlt * 16:(dlt + 1) * 16].unsqueeze(2) \
                        .broadcast_to([128, 16, 64])
                    v3 = v_t[dlt][:, :].rearrange("p (h d) -> p h d", d=64)
                    if dlt == 0:
                        v.tensor_tensor(o3, v3, p3, ALU.mult)
                    else:
                        v.tensor_tensor(m3, v3, p3, ALU.mult)
                        v.tensor_tensor(obuf[:, :], obuf[:, :], mbuf[:, :],
                                        ALU.add)
                r3 = rcl[:, :].unsqueeze(2).broadcast_to([128, 16, 64])
                v.tensor_tensor(o3, o3, r3, ALU.mult)
                ins = v.tensor_copy(obf[:, :], obuf[:, :])
                cnt = ctx.inc("ve", ins)
                tile_free[0] = ("ve", cnt)
            ve_o = ctx.val("ve")

            @blk.tensor
            def _(t, b=b, j=j, ve_o=ve_o):
                ctx.wait(t, "ve", ve_o)
                if j == 0:
                    for c in range(8):
                        ctx.wait_bank_free(t, c)
                ins = None
                for c in range(8):
                    ins = t.matmul(ctx.ps[c][:, j * 128:(j + 1) * 128],
                                   obf[:, c * 128:(c + 1) * 128],
                                   consts["ident"][:, :], start=True, stop=True)
                cnt = ctx.inc("pe", ins)
                obf_free[0] = ("pe", cnt)
                if j == 3:
                    for c in range(8):
                        ctx.set_bank_ready(c, "pe", cnt)

        @blk.vector
        def _(v, b=b):
            ins = None
            for c in range(8):
                ctx.wait_bank_ready(v, c)
                ins = v.tensor_copy(
                    attnout[:, c * TL + b * 512:c * TL + (b + 1) * 512],
                    ctx.ps[c][:, :])
            cnt = ctx.inc("ve", ins)
            for c in range(8):
                ctx.release_bank(c, "ve", cnt)
    att_done = ("ve", ctx.val("ve"))

    @blk.gpsimd
    def _(g):
        ctx.wait(g, *prod_pe)
        for c in range(NF):
            ins = g.dma_start(out=xln[:, c * TL:(c + 1) * TL],
                              in_=h0_dram[c * 128:(c + 1) * 128, :])
            ctx.inc("h0ld", ins, 16)
    h0_ld = ("h0ld", ctx.val("h0ld"))

    wproj = ctx.at("l1a_wp", [128, 8192], BF16, A_WB0)
    wldp = emit_weight_load(ctx, wproj, dr["l1a_projw"], 8, 1024, gate=prod_pe)

    def cb_proj(fo, banks):
        @blk.vector
        def _(v, fo=fo, banks=banks):
            ctx.wait(v, *h0_ld)
            ins = None
            for tt in range(NT):
                ctx.wait_bank_ready(v, banks[tt])
                sl = slice(fo * TL + tt * 512, fo * TL + (tt + 1) * 512)
                v.tensor_tensor(xln[:, sl], xln[:, sl], ctx.ps[banks[tt]][:, :],
                                ALU.add)
                ins = v.tensor_tensor(x_sb[:, sl], x_sb[:, sl],
                                      ctx.ps[banks[tt]][:, :], ALU.add)
            cnt = ctx.inc("ve", ins)
            for tt in range(NT):
                ctx.release_bank(banks[tt], "ve", cnt)

    emit_proj(ctx, attnout, 0, NF, wproj, 1024, list(range(8)), TL, cb_proj,
              bias_row=bpr, consts=consts, wld_target=wldp, src_ready=att_done)
    x1_done = ("ve", ctx.val("ve"))

    @blk.gpsimd
    def _(g):
        ctx.wait(g, *x1_done)
        for c in range(NF):
            ins = g.dma_start(out=y_dram[c * 128:(c + 1) * 128, :],
                              in_=x_sb[:, c * TL:(c + 1) * TL])
            ctx.inc("st", ins, 16)

    emit_ln(ctx, st, src_wait=[x1_done, ("pe", ctx.val("pe"))], src=xln,
            dst=xln, affine=(lambda c: g2c[:, c:c + 1],
                             lambda c: b2c[:, c:c + 1]), inplace=True)
    ln2_done = ("se", ctx.val("se"))

    wb = [ctx.at("l1a_fw0", [128, 8192], BF16, A_WB0),
          ctx.at("l1a_fw1", [128, 8192], BF16, A_WB1)]
    ycb = ctx.at("l1a_ycb", [128, 2048], BF16, TA)
    xcbuf = ctx.at("l1a_xcb", [128, 4096], F32, TA + 4 * KB)
    har0 = ctx.at("l1a_har0", [128, 32768], BF16, OFF_X)
    har1 = ctx.at("l1a_har1", [128, 16384], BF16, OFF_XLN)
    har2 = ctx.at("l1a_har2", [128, 16384], BF16, OFF_LNS)

    def harc(c):
        if c < 16:
            return har0[:, c * TL:(c + 1) * TL]
        if c < 24:
            return har1[:, (c - 16) * TL:(c - 15) * TL]
        return har2[:, (c - 24) * TL:(c - 23) * TL]

    def hsrc(fi, tt):
        if fi < 16:
            return har0[:, fi * TL + tt * 512:fi * TL + (tt + 1) * 512]
        if fi < 24:
            f = fi - 16
            return har1[:, f * TL + tt * 512:f * TL + (tt + 1) * 512]
        f = fi - 24
        return har2[:, f * TL + tt * 512:f * TL + (tt + 1) * 512]

    def xlnsrc(fi, tt):
        return xln[:, fi * TL + tt * 512:fi * TL + (tt + 1) * 512]

    ycb_state = {"free": None}

    def cbF1(fo, banks):
        @blk.scalar
        def _(s, fo=fo, banks=banks):
            if ycb_state["free"] is not None:
                ctx.wait(s, *ycb_state["free"])
            ins = None
            for tt in range(NT):
                ctx.wait_bank_ready(s, banks[tt])
                ins = s.activation(ycb[:, tt * 512:(tt + 1) * 512],
                                   ctx.ps[banks[tt]][:, :], AF.Gelu,
                                   bias=fb1c[:, fo:fo + 1])
            cnt = ctx.inc("se", ins)
            for tt in range(NT):
                ctx.release_bank(banks[tt], "se", cnt)

        @blk.sync
        def _(sy, fo=fo):
            ctx.wait(sy, "se")
            ins = sy.dma_start(out=h_dram[fo * 128:(fo + 1) * 128, :],
                               in_=ycb[:, 0:TL])
            cnt = ctx.inc("st", ins, 16)
            ycb_state["free"] = ("st", cnt)

    emit_bigproj(ctx, xlnsrc, 8, dr["l1a_fc1"], 32, TL, cbF1, wb, 8,
                 src_ready=ln2_done)
    f1_pe = ("pe", ctx.val("pe"))

    @blk.gpsimd
    def _(g):
        ctx.wait(g, *f1_pe)
        ctx.wait(g, "st")
        for c in range(32):
            ins = g.dma_start(out=harc(c), in_=h_dram[c * 128:(c + 1) * 128, :])
            ctx.inc("g1ld", ins, 16)
    h_ld = ("g1ld", ctx.val("g1ld"))

    xc_ready = {}

    @blk.sync
    def _(sy):
        ctx.wait(sy, "st")
        for fo in range(2):
            ins = sy.dma_start(out=xcbuf[:, fo * TL:(fo + 1) * TL],
                               in_=y_dram[fo * 128:(fo + 1) * 128, :])
            xc_ready[fo] = ("xld", ctx.inc("xld", ins, 16))

    def cbF2(fo, banks):
        half = fo % 2

        @blk.vector
        def _(v, fo=fo, banks=banks, half=half):
            ctx.wait(v, *xc_ready[fo])
            ins = None
            for tt in range(NT):
                ctx.wait_bank_ready(v, banks[tt])
                sl = slice(half * TL + tt * 512, half * TL + (tt + 1) * 512)
                v.tensor_tensor(xcbuf[:, sl], xcbuf[:, sl],
                                ctx.ps[banks[tt]][:, :], ALU.add)
                ins = v.tensor_scalar(xcbuf[:, sl], xcbuf[:, sl],
                                      fb2c[:, fo:fo + 1], None, ALU.add)
            cnt = ctx.inc("ve", ins)
            for tt in range(NT):
                ctx.release_bank(banks[tt], "ve", cnt)

        @blk.sync
        def _(sy, fo=fo, half=half):
            ctx.wait(sy, "ve")
            ins = sy.dma_start(out=y_dram[fo * 128:(fo + 1) * 128, :],
                               in_=xcbuf[:, half * TL:(half + 1) * TL])
            cnt = ctx.inc("st", ins, 16)
            if fo + 2 < 8:
                ctx.wait(sy, "st", cnt)
                ins2 = sy.dma_start(out=xcbuf[:, half * TL:(half + 1) * TL],
                                    in_=y_dram[(fo + 2) * 128:(fo + 3) * 128, :])
                xc_ready[fo + 2] = ("xld", ctx.inc("xld", ins2, 16))

    emit_bigproj(ctx, hsrc, 32, dr["l1a_fc2"], 8, TL, cbF2, wb, 2,
                 src_ready=h_ld)

    @blk.sync
    def _(sy):
        ctx.wait(sy, "st")



def weights_l2_attn_a(inp):
    return {
        "l2a_Wq": _np(inp["L2_attn_a_Wq"]),
        "l2a_Wk": _np(inp["L2_attn_a_Wk"]),
        "l2a_Wv": _np(inp["L2_attn_a_Wv"]),
        "l2a_Wo": _np(inp["L2_attn_a_Wo"]),
        "l2a_gamma": _row(inp["L2_attn_a_gamma"]),
    }


def emit_l2_attn_a(ctx, st, core):
    blk = ctx.block
    nc = ctx.nc
    dr = st["dram"]
    consts = st["consts"]
    x_sb, xln = st["x"], st["xln"]

    k_dram = nc.dram_tensor("l2a_K", [D, T], BF16).ap()
    v_dram = nc.dram_tensor("l2a_V", [T, D], BF16).ap()

    wk = ctx.at("l2a_wk", [128, 8192], BF16, A_WB0)
    wv = ctx.at("l2a_wv", [128, 8192], BF16, A_WB1)
    stage = ctx.at("l2a_stage", [128, 16384], BF16, A_QFM)
    vstg = ctx.at("l2a_vstg", [128, 2048], BF16, A_VSTG)
    grow = ctx.at("l2a_grow", [1, 16], BF16, A_MISC)
    gcol = ctx.at("l2a_gcol", [128, 16], F32, A_MISC + 64)
    ones_t = ctx.at("l2a_onest", [128, 512], BF16, A_MISC + 384)

    @blk.vector
    def _(v):
        ins = v.memset(ones_t[:, :], 1.0)
        ctx.inc("ve", ins)

    emit_weight_load(ctx, wk, dr["l2a_Wk"], 8, 1024)
    emit_weight_load(ctx, wv, dr["l2a_Wv"], 8, 1024)
    wld0 = emit_row_load(ctx, grow[0:1, :], dr["l2a_gamma"][0:1, :])

    @blk.tensor
    def _(t):
        ctx.wait(t, "wld", wld0)
        ctx.wait_bank_free(t, 6)
        ins = t.matmul(ctx.ps[6][:, 0:16], consts["ones_row128"][0:1, :],
                       grow[0:1, 0:16], start=True, stop=True)
        cnt = ctx.inc("pe", ins)
        ctx.set_bank_ready(6, "pe", cnt)

    @blk.vector
    def _(v):
        ctx.wait_bank_ready(v, 6)
        v.tensor_scalar(gcol[:, :], ctx.ps[6][:, 0:16], 0.25, None, ALU.mult)
        ins = v.tensor_copy(vstg[:, 0:512], ctx.ps[6][:, :])
        cnt = ctx.inc("ve", ins)
        ctx.release_bank(6, "ve", cnt)

    emit_load_x(ctx, x_sb, st["xp_dram"])
    emit_ln(ctx, st, src_wait=("ld", ctx.val("ld")))
    ln1 = ("ve", ctx.val("ve"))
    emit_kfm_to_dram(ctx, st, wk, wld0, NF, k_dram, core, stage,
                     src_ready=ln1, consts=consts, half=(core ^ 1) % 2)
    emit_vtok_to_dram(ctx, st, wv, wld0, D, v_dram, core, vstg,
                      src_ready=ln1, consts=consts, half=(core ^ 1) % 2)
    p1_pe = ("pe", ctx.val("pe"))
    p1_st = ctx.val("st")

    emit_load_x2(ctx, x_sb, st["x_dram"], gate=p1_pe)
    emit_ln(ctx, st, src_wait=[("ld", ctx.val("ld")), p1_pe,
                               ("st", ctx.val("st"))])
    ln2 = ("ve", ctx.val("ve"))
    emit_kfm_to_dram(ctx, st, wk, wld0, NF, k_dram, core, stage,
                     src_ready=ln2, consts=consts, half=core % 2,
                     stage_gate=("st", p1_st))
    emit_vtok_to_dram(ctx, st, wv, wld0, D, v_dram, core, vstg,
                      src_ready=ln2, consts=consts, half=core % 2,
                      stg_gate=("st", p1_st))
    kv_done_st = ctx.val("st")
    p2_pe = ("pe", ctx.val("pe"))

    wq = ctx.at("l2a_wq", [128, 8192], BF16, A_WB0)
    wldq = emit_weight_load(ctx, wq, dr["l2a_Wq"], 8, 1024, gate=p2_pe)
    qfm = ctx.at("l2a_qfm", [128, 16384], BF16, A_QFM)
    emit_proj(ctx, xln, 0, NF, wq, 1024, list(range(8)), TL,
              evac_simple(ctx, qfm, 0, TL, eng="vector"),
              consts=consts, wld_target=wldq, src_ready=("st", kv_done_st))
    q_done = ("ve", ctx.val("ve"))

    band = ctx.at("l2a_band", [128, 4096], BF16, A_WB0)
    wldband = emit_row_load(ctx, band[:, :], dr["l2a_band"][:, :], gate=q_done)
    ksq = ctx.at("l2a_ksq", [128, 4096], BF16, A_WB0 + 8 * KB)
    qsq = ctx.at("l2a_qsq", [128, 2048], BF16, A_WB1)
    kn_bf = ctx.at("l2a_kn", [128, 4096], BF16, A_WB1 + 4 * KB)
    qn_bf = ctx.at("l2a_qn", [128, 2048], BF16, A_WB1 + 12 * KB)
    khb_al = [ctx.at("l2a_khb0", [128, 4096], BF16, A_KH),
              ctx.at("l2a_khb1", [128, 4096], BF16, A_KH + 8 * KB)]

    def head_pre(h, slot, kld_now, user_gate):
        pb = (h % 2) * 64

        @blk.scalar
        def _(s, h=h, pb=pb, slot=slot, kld_now=kld_now, user_gate=user_gate):
            ctx.wait(s, "kld", kld_now)
            if user_gate is not None:
                ctx.wait(s, *user_gate)
            s.activation(ksq[pb:pb + 64, :], khb_al[slot][pb:pb + 64, :],
                         AF.Square)
            qc = (h // 2) * TL
            ins = s.activation(qsq[pb:pb + 64, :], qfm[pb:pb + 64, qc:qc + TL],
                               AF.Square)
            ctx.inc("se", ins)
        se_sq = ctx.val("se")

        for j in range(12):
            bank = 6 + (j % 2)

            @blk.tensor
            def _(t, j=j, bank=bank, pb=pb, se_sq=se_sq):
                ctx.wait(t, "se", se_sq)
                ctx.wait_bank_free(t, bank)
                if j < 8:
                    src = ksq[pb:pb + 64, j * 512:(j + 1) * 512]
                else:
                    src = qsq[pb:pb + 64, (j - 8) * 512:(j - 7) * 512]
                ins = t.matmul(ctx.ps[bank][0:1, :],
                               consts["ones_col"][pb:pb + 64, 0:1], src,
                               start=True, stop=True)
                cnt = ctx.inc("pe", ins)
                ctx.set_bank_ready(bank, "pe", cnt)

            @blk.vector
            def _(v, j=j, bank=bank, pb=pb):
                ctx.wait_bank_ready(v, bank)
                if j < 8:
                    dst = kn_bf[pb:pb + 1, j * 512:(j + 1) * 512]
                else:
                    dst = qn_bf[pb:pb + 1, (j - 8) * 512:(j - 7) * 512]
                ins = v.tensor_scalar(dst, ctx.ps[bank][0:1, :], -0.5, None,
                                      ALU.mult)
                cnt = ctx.inc("ve", ins)
                ctx.release_bank(bank, "ve", cnt)
        return ("ve", ctx.val("ve"))

    def ksub_rank1(h, qt, ksub):
        pb = (h % 2) * 64
        return [
            (kn_bf[pb:pb + 1, ksub * 128:(ksub + 1) * 128],
             ones_t[pb:pb + 1, 0:512]),
            (ones_t[pb:pb + 1, 0:128],
             qn_bf[pb:pb + 1, qt * 512:(qt + 1) * 512]),
        ]

    def kspec(qt, ksub):
        s = ksub - 8 * qt
        if s < 0:
            return "skip"
        if s >= 8:
            return "full"
        return ("bias", band[:, s * 512:(s + 1) * 512])

    attnout = ctx.at("l2a_attnout", [128, 16384], BF16, OFF_XLN)
    cfg = {
        "H": 16, "dh": 64, "k_dram": k_dram, "v_dram": v_dram,
        "exp_scale": lambda h: gcol[:, h:h + 1], "norm": False,
        "out_sb": attnout, "qfm": qfm, "kv_ready": kv_done_st,
        "q_slice": lambda h, qt: qfm[(h % 2) * 64:(h % 2) * 64 + 64,
                                     (h // 2) * TL + qt * 512:(h // 2) * TL + (qt + 1) * 512],
        "out_base": lambda h: (h // 2, (h % 2) * 64),
        "q_ready": q_done,
        "kld_gate": q_done,
        "kld_gate2": ("wld", wldband),
        "pre_gate": ("pe", ctx.val("pe")),
        "ksub_spec": kspec,
        "ksub_rank1": ksub_rank1,
        "head_pre": head_pre,
    }
    emit_attn_core(ctx, st, cfg)
    att_done = ("ve", ctx.val("ve"))

    if DEBUG_TAPS:
        nc2 = ctx.nc
        dbg_k = nc2.dram_tensor("dbg_k", [D, T], F32, kind="ExternalOutput").ap()
        dbg_v = nc2.dram_tensor("dbg_v", [T, D], F32, kind="ExternalOutput").ap()
        dbg_q = nc2.dram_tensor("dbg_q", [D, TL], F32, kind="ExternalOutput").ap()
        dbg_att = nc2.dram_tensor("dbg_att", [D, TL], F32,
                                  kind="ExternalOutput").ap()
        dbg_kn = nc2.dram_tensor("dbg_kn", [128, 4096], F32,
                                 kind="ExternalOutput").ap()
        dbg_qn = nc2.dram_tensor("dbg_qn", [128, 2048], F32,
                                 kind="ExternalOutput").ap()

        @blk.gpsimd
        def _(g):
            ctx.wait(g, "ve")
            ins = g.dma_start(out=dbg_k[:, :], in_=k_dram[:, :])
            ctx.inc("st", ins, 16)
            ins = g.dma_start(out=dbg_v[:, :], in_=v_dram[:, :])
            ctx.inc("st", ins, 16)
            ins = g.dma_start(out=dbg_kn[:, :], in_=kn_bf[:, :])
            ctx.inc("st", ins, 16)
            ins = g.dma_start(out=dbg_qn[:, :], in_=qn_bf[:, :])
            ctx.inc("st", ins, 16)
            for c in range(NF):
                ins = g.dma_start(out=dbg_q[c * 128:(c + 1) * 128, :],
                                  in_=qfm[:, c * TL:(c + 1) * TL])
                ctx.inc("st", ins, 16)
                ins = g.dma_start(out=dbg_att[c * 128:(c + 1) * 128, :],
                                  in_=attnout[:, c * TL:(c + 1) * TL])
                ctx.inc("st", ins, 16)

    wo = ctx.at("l2a_wo", [128, 8192], BF16, A_WB1)
    wldo = emit_weight_load(ctx, wo, dr["l2a_Wo"], 8, 1024, gate=att_done)
    emit_proj(ctx, attnout, 0, NF, wo, 1024, list(range(8)), TL,
              evac_resid(ctx, x_sb), consts=consts,
              wld_target=wldo, src_ready=att_done)



L1B_C = 128
L1B_G = 512


def weights_l1_attn_b(inp):
    half = 64
    inv = 1.0 / (10000.0 ** (np.arange(half, dtype=np.float64) / half))
    t = np.arange(T, dtype=np.float64)
    fr = t[None, :] * inv[:, None]
    cs = np.repeat(np.cos(fr), 2, axis=0).astype(np.float32)
    sn = np.repeat(np.sin(fr), 2, axis=0).astype(np.float32)
    sn[0::2, :] *= -1.0
    ctri = np.triu(np.ones((128, 128), dtype=np.float32))
    e8 = np.zeros((8, 1024), dtype=np.float32)
    for h in range(8):
        e8[h, h * 128:(h + 1) * 128] = 1.0
    return {
        "l1b_wq": _tr(inp["L1_attn_b_Wq"]),
        "l1b_wk": _tr(inp["L1_attn_b_Wk"]),
        "l1b_wv": _tr(inp["L1_attn_b_Wv"]),
        "l1b_wo": _tr(inp["L1_attn_b_Wo"]),
        "l1b_wb": _tr(inp["L1_attn_b_Wbeta"]),
        "l1b_wa": _tr(inp["L1_attn_b_Walpha"]),
        "l1b_bb": np.asarray(inp["L1_attn_b_bbeta"],
                             dtype=np.float32).reshape(8, 1),
        "l1b_ba": np.asarray(inp["L1_attn_b_balpha"],
                             dtype=np.float32).reshape(8, 1),
        "l1b_bo": _row(inp["L1_attn_b_bo"]),
        "l1b_cs": cs, "l1b_sn": sn, "l1b_ctri": ctri, "l1b_e8": e8,
    }


def emit_l1_attn_b(ctx, st, core):
    blk = ctx.block
    nc = ctx.nc
    dr = st["dram"]
    consts = st["consts"]
    xfull, y_dram = st["x_dram"], st["y_dram"]

    wq_sb = ctx.at("l1b_wqs", [128, 8192], BF16, A_WB0)
    wk_sb = ctx.at("l1b_wks", [128, 8192], BF16, A_WB1)
    wv_sb = ctx.at("l1b_wvs", [128, 8192], BF16, OFF_XLN)
    wo_sb = ctx.at("l1b_wos", [128, 8192], BF16, OFF_XLN + 16 * KB)
    wb_sb = ctx.at("l1b_wbs", [128, 64], BF16, OFF_MISC)
    wa_sb = ctx.at("l1b_was", [128, 64], BF16, OFF_MISC + 256)
    bbcol = ctx.at("l1b_bbc", [8, 1], F32, OFF_MISC + 512)
    bacol = ctx.at("l1b_bac", [8, 1], F32, OFF_MISC + 576)
    borow = ctx.at("l1b_bor", [1, 1024], BF16, OFF_MISC + 640)
    brow = ctx.at("l1b_brow", [8, 1024], F32, OFF_MISC + 2688)

    xg = [ctx.at("l1b_xg0", [128, 4096], F32, OFF_X),
          ctx.at("l1b_xg1", [128, 4096], F32, OFF_X + 16 * KB)]
    cs_sb = ctx.at("l1b_css", [128, 4096], BF16, OFF_X + 32 * KB)
    sn_sb = ctx.at("l1b_sns", [128, 4096], BF16, OFF_X + 40 * KB)
    xlns = [ctx.at("l1b_xln0", [128, 4096], BF16, OFF_X + 48 * KB),
            ctx.at("l1b_xln1", [128, 4096], BF16, OFF_X + 56 * KB)]

    sq2 = ctx.at("l1b_sq", [128, 1024], BF16, OFF_LNS)
    rows2 = ctx.at("l1b_rows", [1, 1536], F32, OFF_LNS + 2 * KB)
    abrow2 = ctx.at("l1b_abrow", [1, 1024], BF16, OFF_LNS + 8 * KB)
    qf = ctx.at("l1b_qf", [128, 4096], BF16, OFF_LNS + 10 * KB)
    kf = ctx.at("l1b_kf", [128, 4096], BF16, OFF_LNS + 18 * KB)

    TA = OFF_TAIL3
    out_fm = ctx.at("l1b_out", [128, 4096], BF16, TA)
    vt = ctx.at("l1b_vt", [128, 4096], BF16, TA + 8 * KB)
    sf = ctx.at("l1b_sf", [128, 1024], F32, TA + 16 * KB)
    sbf = ctx.at("l1b_sbf", [128, 1024], BF16, TA + 20 * KB)
    pbuf = ctx.at("l1b_p", [128, 128], BF16, TA + 22 * KB)
    ktok = ctx.at("l1b_ktok", [128, 128], BF16, TA + 22 * KB + 256)
    dcc = ctx.at("l1b_dcc", [128, 1], F32, TA + 22 * KB + 512)
    rtmp = ctx.at("l1b_rtmp", [128, 512], BF16, TA + 23 * KB)
    e8 = ctx.at("l1b_e8s", [8, 1024], BF16, TA + 24 * KB)
    ctri = ctx.at("l1b_ctri", [128, 128], BF16, TA + 26 * KB)
    zrow = ctx.at("l1b_zrow", [8, 128], F32, TA + 26 * KB + 256)
    drow = ctx.at("l1b_drow", [8, 512], F32, TA + 27 * KB)
    kscl = ctx.at("l1b_kscl", [8, 512], F32, TA + 29 * KB)
    dbf = ctx.at("l1b_dbf", [8, 512], BF16, TA + 31 * KB)
    ksbf = ctx.at("l1b_ksbf", [8, 512], BF16, TA + 32 * KB)

    st2 = dict(st)
    st2["sq"], st2["rows"], st2["abrow"] = sq2, rows2, abrow2

    emit_weight_load(ctx, wq_sb, dr["l1b_wq"], 8, 1024)
    emit_weight_load(ctx, wk_sb, dr["l1b_wk"], 8, 1024)
    emit_weight_load(ctx, wv_sb, dr["l1b_wv"], 8, 1024)
    emit_weight_load(ctx, wo_sb, dr["l1b_wo"], 8, 1024)
    emit_weight_load(ctx, wb_sb, dr["l1b_wb"], 8, 8)
    emit_weight_load(ctx, wa_sb, dr["l1b_wa"], 8, 8)
    for tile, name in ((bbcol[0:8, :], "l1b_bb"), (bacol[0:8, :], "l1b_ba"),
                       (borow[0:1, :], "l1b_bo"), (cs_sb[:, :], "l1b_cs"),
                       (sn_sb[:, :], "l1b_sn"), (e8[0:8, :], "l1b_e8"),
                       (ctri[:, :], "l1b_ctri")):
        emit_row_load(ctx, tile, dr[name])
    wld0 = ctx.val("wld")

    @blk.vector
    def _(v):
        v.memset(sf[:, :], 0.0)
        v.memset(sbf[:, :], 0.0)
        ins = v.memset(zrow[0:8, :], 1.0)
        ctx.inc("ve", ins)
    sdone = [("ve", ctx.val("ve"))] * 8
    sdone = list(sdone)

    xg_free = [None, None]
    xln_users = [None, None]
    scan_done = None
    outp_pe = None

    NG = T // L1B_G

    for g in range(NG):
        slot = g % 2

        @blk.sync
        def _(sy, g=g, slot=slot):
            if xg_free[slot] is not None:
                ctx.wait(sy, *xg_free[slot])
            for c in range(NF):
                ins = sy.dma_start(
                    out=xg[slot][:, c * 512:(c + 1) * 512],
                    in_=xfull[c * 128:(c + 1) * 128, g * 512:(g + 1) * 512])
                ctx.inc("ld", ins, 16)
        ld_now = ctx.val("ld")

        lnw = [("ld", ld_now)]
        if xln_users[slot] is not None:
            lnw.append(xln_users[slot])
        emit_ln(ctx, st2, tl=512, nf=8, src_wait=lnw, src=xg[slot],
                dst=xlns[slot])
        ln_done = ("ve", ctx.val("ve"))

        sr = [ln_done]
        if scan_done is not None:
            sr.append(scan_done)
        emit_proj(ctx, xlns[slot], 0, 8, wq_sb, 1024, list(range(8)), 512,
                  evac_simple(ctx, qf, 0, 512, eng="vector"), consts=consts,
                  wld_target=wld0, src_ready=sr)
        emit_proj(ctx, xlns[slot], 0, 8, wk_sb, 1024, list(range(8)), 512,
                  evac_simple(ctx, kf, 0, 512, eng="vector"), consts=consts,
                  wld_target=wld0, src_ready=sr)

        for tj in range(4):
            @blk.tensor
            def _(t, tj=tj, slot=slot, sr=sr):
                ctx.wait(t, "wld", wld0)
                for w in sr:
                    ctx.wait(t, *w)
                for bk in (6, 7):
                    ctx.wait_bank_free(t, bk)
                ins = None
                for fi in range(NF):
                    for gv in range(2):
                        ins = t.matmul(
                            ctx.ps[6 + gv][:, :],
                            xlns[slot][:, fi * 512 + tj * 128:
                                       fi * 512 + (tj + 1) * 128],
                            wv_sb[:, fi * 1024 + gv * 512:
                                  fi * 1024 + (gv + 1) * 512],
                            start=(fi == 0), stop=(fi == NF - 1))
                cnt = ctx.inc("pe", ins)
                for bk in (6, 7):
                    ctx.set_bank_ready(bk, "pe", cnt)

            @blk.vector
            def _(v, tj=tj):
                ins = None
                for gv in range(2):
                    ctx.wait_bank_ready(v, 6 + gv)
                    ins = v.tensor_copy(
                        vt[:, tj * 1024 + gv * 512:tj * 1024 + (gv + 1) * 512],
                        ctx.ps[6 + gv][:, :])
                cnt = ctx.inc("ve", ins)
                for bk in (6, 7):
                    ctx.release_bank(bk, "ve", cnt)
        vt_done = ("ve", ctx.val("ve"))

        @blk.tensor
        def _(t, slot=slot, sr=sr):
            ctx.wait(t, "wld", wld0)
            for w in sr:
                ctx.wait(t, *w)
            for bk in (2, 3):
                ctx.wait_bank_free(t, bk)
            ins = None
            for fi in range(NF):
                t.matmul(ctx.ps[2][0:8, :], wb_sb[:, fi * 8:(fi + 1) * 8],
                         xlns[slot][:, fi * 512:(fi + 1) * 512],
                         start=(fi == 0), stop=(fi == NF - 1))
                ins = t.matmul(ctx.ps[3][0:8, :], wa_sb[:, fi * 8:(fi + 1) * 8],
                               xlns[slot][:, fi * 512:(fi + 1) * 512],
                               start=(fi == 0), stop=(fi == NF - 1))
            cnt = ctx.inc("pe", ins)
            for bk in (2, 3):
                ctx.set_bank_ready(bk, "pe", cnt)

        @blk.scalar
        def _(s):
            ctx.wait_bank_ready(s, 2)
            ctx.wait_bank_ready(s, 3)
            s.activation(brow[0:8, 0:512], ctx.ps[2][0:8, :], AF.Sigmoid,
                         bias=bbcol[0:8, 0:1])
            ins = s.activation(brow[0:8, 512:1024], ctx.ps[3][0:8, :],
                               AF.Sigmoid, bias=bacol[0:8, 0:1])
            ctx.inc("se", ins)
        se_ba = ctx.val("se")

        @blk.vector
        def _(v, se_ba=se_ba):
            ctx.wait(v, "se", se_ba)
            v.tensor_scalar(brow[0:8, 0:512], brow[0:8, 0:512], 0.149, 0.85,
                            ALU.mult, ALU.add)
            v.tensor_copy(rtmp[:, 0:512], cs_sb[:, 0:512])
            for j in range(4):
                v.tensor_tensor_scan(drow[0:8, j * 128:(j + 1) * 128],
                                     brow[0:8, j * 128:(j + 1) * 128],
                                     zrow[0:8, :], 1.0, ALU.mult, ALU.mult)
            v.tensor_copy(rtmp[:, 0:512], cs_sb[:, 0:512])
            v.reciprocal(kscl[0:8, :], drow[0:8, :])
            v.tensor_copy(rtmp[:, 0:512], cs_sb[:, 0:512])
            v.tensor_tensor(kscl[0:8, :], kscl[0:8, :], brow[0:8, 512:1024],
                            ALU.mult)
            v.tensor_copy(rtmp[:, 0:512], cs_sb[:, 0:512])
            v.tensor_copy(dbf[0:8, :], drow[0:8, :])
            ins = v.tensor_copy(ksbf[0:8, :], kscl[0:8, :])
            cnt = ctx.inc("ve", ins)
            for bk in (2, 3):
                ctx.release_bank(bk, "ve", cnt)
        rows_done = ("ve", ctx.val("ve"))

        for h in range(8):
            @blk.tensor
            def _(t, h=h, rows_done=rows_done):
                ctx.wait(t, *rows_done)
                for bk in (4, 5):
                    ctx.wait_bank_free(t, bk)
                t.matmul(ctx.ps[4][:, :], e8[0:8, h * 128:(h + 1) * 128],
                         dbf[0:8, 0:512], start=True, stop=True)
                ins = t.matmul(ctx.ps[5][:, :], e8[0:8, h * 128:(h + 1) * 128],
                               ksbf[0:8, 0:512], start=True, stop=True)
                cnt = ctx.inc("pe", ins)
                for bk in (4, 5):
                    ctx.set_bank_ready(bk, "pe", cnt)

            @blk.vector
            def _(v, h=h, g=g):
                for bk in (4, 5):
                    ctx.wait_bank_ready(v, bk)
                css = cs_sb[:, g * 512:(g + 1) * 512]
                sns = sn_sb[:, g * 512:(g + 1) * 512]
                for buf, bscale in ((qf, 4), (kf, 5)):
                    sl = buf[:, h * 512:(h + 1) * 512]
                    v.stream_shuffle(rtmp[:, :], sl, XMASKS[0])
                    v.tensor_tensor(sl, sl, css, ALU.mult)
                    v.tensor_tensor(rtmp[:, :], rtmp[:, :], sns, ALU.mult)
                    v.tensor_tensor(sl, sl, rtmp[:, :], ALU.add)
                    v.tensor_tensor(sl, sl, ctx.ps[bscale][:, :], ALU.mult)
                ins = v.tensor_copy(rtmp[:, 0:512], cs_sb[:, 0:512])
                cnt = ctx.inc("ve", ins)
                for bk in (4, 5):
                    ctx.release_bank(bk, "ve", cnt)
        qk_done = ("ve", ctx.val("ve"))

        for cc in range(4):
            for h in range(8):
                idx = cc * 8 + h
                bA, bT = idx % 2, 4 + idx % 2
                bO, bS = 2 + idx % 2, 6 + idx % 2
                qsl = qf[:, h * 512 + cc * 128:h * 512 + (cc + 1) * 128]
                ksl = kf[:, h * 512 + cc * 128:h * 512 + (cc + 1) * 128]
                vsl = vt[:, cc * 1024 + h * 128:cc * 1024 + (h + 1) * 128]

                @blk.tensor
                def _(t, bA=bA, bT=bT, qsl=qsl, ksl=ksl, qk_done=qk_done,
                      vt_done=vt_done):
                    ctx.wait(t, *qk_done)
                    ctx.wait(t, *vt_done)
                    ctx.wait_bank_free(t, bA)
                    ctx.wait_bank_free(t, bT)
                    t.matmul(ctx.ps[bA][:, 0:128], ksl, qsl, start=True,
                             stop=True)
                    ins = t.matmul(ctx.ps[bT][:, 0:128], ksl,
                                   consts["ident"][:, :], start=True, stop=True)
                    cnt = ctx.inc("pe", ins)
                    ctx.set_bank_ready(bA, "pe", cnt)
                    ctx.set_bank_ready(bT, "pe", cnt)

                @blk.vector
                def _(v, bA=bA, bT=bT):
                    ctx.wait_bank_ready(v, bA)
                    ctx.wait_bank_ready(v, bT)
                    v.tensor_tensor(pbuf[:, :], ctx.ps[bA][:, 0:128],
                                    ctri[:, :], ALU.mult)
                    ins = v.tensor_copy(ktok[:, :], ctx.ps[bT][:, 0:128])
                    cnt = ctx.inc("ve", ins)
                    ctx.release_bank(bA, "ve", cnt)
                    ctx.release_bank(bT, "ve", cnt)
                pk_ve = ctx.val("ve")

                @blk.tensor
                def _(t, bO=bO, bS=bS, h=h, cc=cc, qsl=qsl, vsl=vsl,
                      pk_ve=pk_ve, sgate=sdone[h], rows_done=rows_done):
                    ctx.wait(t, "ve", pk_ve)
                    ctx.wait(t, *sgate)
                    ctx.wait(t, *rows_done)
                    ctx.wait_bank_free(t, bO)
                    ctx.wait_bank_free(t, bS)
                    t.matmul(ctx.ps[bO][:, 0:128],
                             sbf[:, h * 128:(h + 1) * 128], qsl,
                             start=True, stop=False)
                    t.matmul(ctx.ps[bO][:, 0:128], vsl, pbuf[:, :],
                             start=False, stop=True)
                    t.matmul(ctx.ps[bS][:, 0:128], ktok[:, :], vsl,
                             start=True, stop=True)
                    ins = t.matmul(ctx.ps[bS][:, 128:129],
                                   e8[0:8, h * 128:(h + 1) * 128],
                                   dbf[0:8, cc * 128 + 127:cc * 128 + 128],
                                   start=True, stop=True)
                    cnt = ctx.inc("pe", ins)
                    ctx.set_bank_ready(bO, "pe", cnt)
                    ctx.set_bank_ready(bS, "pe", cnt)

                @blk.vector
                def _(v, bO=bO, bS=bS, h=h, cc=cc, og=outp_pe):
                    ctx.wait_bank_ready(v, bO)
                    ctx.wait_bank_ready(v, bS)
                    if og is not None:
                        ctx.wait(v, *og)
                    v.tensor_copy(out_fm[:, h * 512 + cc * 128:
                                         h * 512 + (cc + 1) * 128],
                                  ctx.ps[bO][:, 0:128])
                    v.tensor_copy(dcc[:, 0:1], ctx.ps[bS][:, 128:129])
                    v.tensor_copy(rtmp[:, 0:512], cs_sb[:, 0:512])
                    v.tensor_tensor(sf[:, h * 128:(h + 1) * 128],
                                    sf[:, h * 128:(h + 1) * 128],
                                    ctx.ps[bS][:, 0:128], ALU.add)
                    v.tensor_scalar(sf[:, h * 128:(h + 1) * 128],
                                    sf[:, h * 128:(h + 1) * 128],
                                    dcc[:, 0:1], None, ALU.mult)
                    ins = v.tensor_copy(sbf[:, h * 128:(h + 1) * 128],
                                        sf[:, h * 128:(h + 1) * 128])
                    cnt = ctx.inc("ve", ins)
                    ctx.release_bank(bO, "ve", cnt)
                    ctx.release_bank(bS, "ve", cnt)
                sdone[h] = ("ve", ctx.val("ve"))
        scan_done = ("ve", ctx.val("ve"))

        def cbO(fo, banks, g=g, slot=slot):
            @blk.vector
            def _(v, fo=fo, banks=banks, slot=slot):
                ctx.wait_bank_ready(v, banks[0])
                ins = v.tensor_tensor(xg[slot][:, fo * 512:(fo + 1) * 512],
                                      xg[slot][:, fo * 512:(fo + 1) * 512],
                                      ctx.ps[banks[0]][:, :], ALU.add)
                cnt = ctx.inc("ve", ins)
                ctx.release_bank(banks[0], "ve", cnt)

            @blk.sync
            def _(sy, fo=fo, g=g, slot=slot):
                ctx.wait(sy, "ve")
                ins = sy.dma_start(
                    out=y_dram[fo * 128:(fo + 1) * 128, g * 512:(g + 1) * 512],
                    in_=xg[slot][:, fo * 512:(fo + 1) * 512])
                ctx.inc("st", ins, 16)

        emit_proj(ctx, out_fm, 0, 8, wo_sb, 1024, list(range(8)), 512, cbO,
                  bias_row=borow, consts=consts, wld_target=wld0,
                  src_ready=[scan_done])
        outp_pe = ("pe", ctx.val("pe"))
        xg_free[slot] = ("st", ctx.val("st"))
        xln_users[slot] = outp_pe

    @blk.sync
    def _(sy):
        ctx.wait(sy, "st")



DHID = 2816
NHC = DHID // 128


def _xor_mask(k):
    return [(i ^ k) % 32 for i in range(32)]


XMASKS = [_xor_mask(1), _xor_mask(2), _xor_mask(4), _xor_mask(8)]


def weights_l2_ffn(inp):
    return {
        "l2f_wg": _np(inp["L2_ffn_w_gate"]),
        "l2f_wu": _np(inp["L2_ffn_w_up"]),
        "l2f_wd": _np(inp["L2_ffn_w_down"]),
    }


def _fp4_staircase(v, XN, QB, TI):
    v.tensor_scalar(QB, XN, 2.2, 2.0, ALU.min, ALU.mult)
    v.tensor_copy(TI, QB)
    v.tensor_copy(QB, TI)
    v.tensor_scalar(QB, QB, 0.5, None, ALU.mult)
    for mid in (2.5, 3.5, 5.0, 5.0):
        v.scalar_tensor_tensor(QB, XN, mid, QB, ALU.is_gt, ALU.add)


def emit_fp4_bfly(ctx, XF, SG, MM, TB, TI, QB, dst, gate):
    blk = ctx.block

    @blk.scalar
    def _(s):
        ctx.wait(s, *gate)
        ins = s.activation(SG, XF, AF.Sign)
        ctx.inc("se", ins)
    se_sg = ctx.val("se")

    @blk.vector
    def _(v):
        ctx.wait(v, "se", se_sg)
        v.tensor_tensor(XF, XF, SG, ALU.mult)
        v.stream_shuffle(TB, XF, XMASKS[0])
        v.tensor_tensor(MM, XF, TB, ALU.max)
        for r in (1, 2, 3):
            v.stream_shuffle(TB, MM, XMASKS[r])
            v.tensor_tensor(MM, MM, TB, ALU.max)
        v.tensor_scalar(MM, MM, 6e-12, None, ALU.max)
        v.reciprocal(TB, MM)
        v.tensor_scalar(TB, TB, 6.0, None, ALU.mult)
        v.tensor_tensor(XF, XF, TB, ALU.mult)
        _fp4_staircase(v, XF, QB, TI)
        v.tensor_scalar(MM, MM, 1.0 / 6.0, None, ALU.mult)
        v.tensor_tensor(QB, QB, SG, ALU.mult)
        ins = v.tensor_tensor(dst, QB, MM, ALU.mult)
        ctx.inc("ve", ins)
    return ctx.val("ve")


def emit_fp4_free(ctx, WF, SG, RD, RB, QB, TI, dst, nb, gate):
    blk = ctx.block

    @blk.scalar
    def _(s):
        ctx.wait(s, *gate)
        ins = s.activation(SG, WF, AF.Sign)
        ctx.inc("se", ins)
    se_sg = ctx.val("se")

    @blk.vector
    def _(v):
        ctx.wait(v, "se", se_sg)
        w3 = WF.rearrange("p (b k) -> p b k", k=16)
        v.tensor_reduce(RD, w3, AX.X, ALU.max, apply_absolute_value=True)
        v.tensor_copy(QB, WF)
        v.tensor_scalar(RD, RD, 6e-12, None, ALU.max)
        v.tensor_copy(QB, WF)
        v.reciprocal(RB, RD)
        v.tensor_copy(QB, WF)
        v.tensor_scalar(RB, RB, 6.0, None, ALU.mult)
        v.tensor_copy(QB, WF)
        v.tensor_tensor(WF, WF, SG, ALU.mult)
        rb3 = RB.unsqueeze(2).broadcast_to([128, nb, 16])
        v.tensor_tensor(w3, w3, rb3, ALU.mult)
        _fp4_staircase(v, WF, QB, TI)
        v.tensor_scalar(RD, RD, 1.0 / 6.0, None, ALU.mult)
        v.tensor_tensor(QB, QB, SG, ALU.mult)
        q3 = QB.rearrange("p (b k) -> p b k", k=16)
        d3 = dst.rearrange("p (b k) -> p b k", k=16)
        rd3 = RD.unsqueeze(2).broadcast_to([128, nb, 16])
        ins = v.tensor_tensor(d3, q3, rd3, ALU.mult)
        ctx.inc("ve", ins)
    return ctx.val("ve")


def emit_l2_ffn(ctx, st, core):
    blk = ctx.block
    nc = ctx.nc
    dr = st["dram"]
    consts = st["consts"]
    x_sb = st["x"]
    x_dram, y_dram = st["x_dram"], st["y_dram"]

    wg_q = nc.dram_tensor("l2f_wgq", [D, DHID], BF16).ap()
    wu_q = nc.dram_tensor("l2f_wuq", [D, DHID], BF16).ap()
    wd_q = nc.dram_tensor("l2f_wdq", [DHID, D], BF16).ap()

    TA = OFF_TAIL3
    XF = ctx.at("l2f_xf", [128, 2048], F32, TA)
    SG = ctx.at("l2f_sg", [128, 2048], F32, TA + 8 * KB)
    MM = ctx.at("l2f_mm", [128, 2048], F32, TA + 16 * KB)
    SIL = ctx.at("l2f_sil", [128, 2048], F32, TA + 16 * KB)
    TB = ctx.at("l2f_tb", [128, 2048], F32, TA + 24 * KB)
    TI = ctx.at("l2f_ti", [128, 2048], I32, TA + 24 * KB)
    QB = ctx.at("l2f_qb", [128, 2048], F32, LNS_SQ)
    xq = ctx.at("l2f_xq", [128, 16384], BF16, OFF_XLN)

    wrow = ctx.at("l2f_wrow", [128, 2816], F32, A_WB0)
    wqb = ctx.at("l2f_wqb", [128, 2816], BF16, A_WB1)
    wtstg = ctx.at("l2f_wtstg", [128, 512], BF16, A_WB1 + 6 * KB)
    RD = ctx.at("l2f_rd", [128, 176], F32, A_WB1 + 8 * KB)
    RB = ctx.at("l2f_rb", [128, 176], F32, A_WB1 + 9 * KB)

    wrow_free = [None]
    wtstg_free = [None]

    def quant_w(src_dram, nch, kcols, q_dram):
        nkb = kcols // 128
        for c in range(nch):
            @blk.sync
            def _(sy, c=c):
                if wrow_free[0] is not None:
                    ctx.wait(sy, *wrow_free[0])
                ins = sy.dma_start(out=wrow[:, 0:kcols],
                                   in_=src_dram[c * 128:(c + 1) * 128, :])
                ctx.inc("wqld", ins, 16)
            ld_now = ctx.val("wqld")

            halves = [(0, kcols)] if kcols <= 2048 else [(0, 1408), (1408, 2816)]
            ve_q = None
            for (a, b) in halves:
                W = b - a
                ve_q = emit_fp4_free(ctx, wrow[:, a:b], SG[:, 0:W],
                                     RD[:, 0:W // 16], RB[:, 0:W // 16],
                                     QB[:, 0:W], TI[:, 0:W], wqb[:, a:b],
                                     W // 16, ("wqld", ld_now))
            wrow_free[0] = ("ve", ve_q)

            for g0 in range(0, nkb, 4):
                gn = min(4, nkb - g0)
                bank = (g0 // 4) % 2

                @blk.tensor
                def _(t, c=c, g0=g0, gn=gn, bank=bank, ve_q=ve_q):
                    ctx.wait(t, "ve", ve_q)
                    ctx.wait_bank_free(t, bank)
                    ins = None
                    for j in range(gn):
                        kb = g0 + j
                        ins = t.matmul(ctx.ps[bank][:, j * 128:(j + 1) * 128],
                                       wqb[:, kb * 128:(kb + 1) * 128],
                                       consts["ident"][:, :], start=True,
                                       stop=True)
                    cnt = ctx.inc("pe", ins)
                    ctx.set_bank_ready(bank, "pe", cnt)

                @blk.vector
                def _(v, bank=bank, gn=gn):
                    ctx.wait_bank_ready(v, bank)
                    if wtstg_free[0] is not None:
                        ctx.wait(v, *wtstg_free[0])
                    ins = v.tensor_copy(wtstg[:, 0:gn * 128],
                                        ctx.ps[bank][:, 0:gn * 128])
                    cnt = ctx.inc("ve", ins)
                    ctx.release_bank(bank, "ve", cnt)
                ve_t = ctx.val("ve")

                @blk.sync
                def _(sy, c=c, g0=g0, gn=gn, ve_t=ve_t):
                    ctx.wait(sy, "ve", ve_t)
                    ins = None
                    for j in range(gn):
                        kb = g0 + j
                        ins = sy.dma_start(
                            out=q_dram[kb * 128:(kb + 1) * 128,
                                       c * 128:(c + 1) * 128],
                            in_=wtstg[:, j * 128:(j + 1) * 128])
                        cnt = ctx.inc("st", ins, 16)
                    wtstg_free[0] = ("st", ctx.val("st"))

    quant_w(dr["l2f_wg"], NHC, 1024, wg_q)
    quant_w(dr["l2f_wu"], NHC, 1024, wu_q)
    quant_w(dr["l2f_wd"], 8, DHID, wd_q)
    wq_ve = ("ve", ctx.val("ve"))
    wq_st = ctx.val("st")

    emit_load_x(ctx, x_sb, x_dram)
    ve_rows = emit_ln(ctx, st, src_wait=[("ld", ctx.val("ld")), wq_ve],
                      dst=xq, stats_only=True)

    @blk.tensor
    def _(t):
        ctx.wait(t, "ve", ve_rows)
        for bk in range(8):
            ctx.wait_bank_free(t, bk)
        ins = None
        for tt in range(NT):
            t.matmul(ctx.ps[tt][:, :], consts["ones_row128"][:, :],
                     st["abrow"][0:1, tt * 512:(tt + 1) * 512],
                     start=True, stop=True)
            ins = t.matmul(ctx.ps[4 + tt][:, :], consts["ones_row128"][:, :],
                           st["abrow"][0:1, TL + tt * 512:TL + (tt + 1) * 512],
                           start=True, stop=True)
        cnt = ctx.inc("pe", ins)
        for bk in range(8):
            ctx.set_bank_ready(bk, "pe", cnt)

    ve_q = None
    for c in range(NF):
        @blk.vector
        def _(v, c=c):
            for bk in range(8):
                ctx.wait_bank_ready(v, bk)
            ins = None
            for tt in range(NT):
                sl = slice(c * TL + tt * 512, c * TL + (tt + 1) * 512)
                xsl = slice(tt * 512, (tt + 1) * 512)
                v.tensor_tensor(XF[:, xsl], x_sb[:, sl], ctx.ps[tt][:, :],
                                ALU.mult)
                ins = v.tensor_tensor(XF[:, xsl], XF[:, xsl],
                                      ctx.ps[4 + tt][:, :], ALU.add)
            ctx.inc("ve", ins)
        ve_q = emit_fp4_bfly(ctx, XF[:, :], SG[:, :], MM[:, :], TB[:, :],
                             TI[:, :], QB[:, :], xq[:, c * TL:(c + 1) * TL],
                             ("ve", ctx.val("ve")))
    for bk in range(8):
        ctx.release_bank(bk, "ve", ve_q)
    xq_done = ("ve", ve_q)

    gar0 = ctx.at("l2f_gar0", [128, 32768], BF16, OFF_X)
    gar1 = ctx.at("l2f_gar1", [128, 12288], BF16, OFF_LNS + 8 * KB)

    def gsl(fo, a, b):
        if fo < 16:
            return gar0[:, fo * TL + a:fo * TL + b]
        return gar1[:, (fo - 16) * TL + a:(fo - 16) * TL + b]

    wgb = [ctx.at("l2f_wgb0", [128, 4096], BF16, A_WB0),
           ctx.at("l2f_wgb1", [128, 4096], BF16, A_WB0 + 8 * KB)]
    wub = [ctx.at("l2f_wub0", [128, 4096], BF16, A_WB1),
           ctx.at("l2f_wub1", [128, 4096], BF16, A_WB1 + 8 * KB)]

    slab_pe = [("pe", ctx.val("pe")), ("pe", ctx.val("pe"))]
    guld = [0] * 6
    hq_ve = None

    for s in range(6):
        nfo = min(4, NHC - s * 4)
        gate_g = slab_pe[s % 2] if s >= 2 else ("st", wq_st)

        @blk.gpsimd
        def _(g, s=s, nfo=nfo, gate_g=gate_g):
            ctx.wait(g, "st", wq_st)
            ctx.wait(g, *gate_g)
            for fi in range(8):
                ins = g.dma_start(
                    out=wgb[s % 2][:, fi * 512:fi * 512 + nfo * 128],
                    in_=wg_q[fi * 128:(fi + 1) * 128,
                             s * 512:s * 512 + nfo * 128])
                ctx.inc("guld", ins, 16)
                ins = g.dma_start(
                    out=wub[s % 2][:, fi * 512:fi * 512 + nfo * 128],
                    in_=wu_q[fi * 128:(fi + 1) * 128,
                             s * 512:s * 512 + nfo * 128])
                ctx.inc("guld", ins, 16)
        guld[s] = ctx.val("guld")

        for fo_l in range(nfo):
            fo = s * 4 + fo_l

            @blk.tensor
            def _(t, s=s, fo_l=fo_l, fo=fo):
                ctx.wait(t, "guld", guld[s])
                ctx.wait(t, *xq_done)
                for bk in range(8):
                    ctx.wait_bank_free(t, bk)
                ins = None
                for fi in range(8):
                    wg_t = wgb[s % 2][:, fi * 512 + fo_l * 128:
                                      fi * 512 + (fo_l + 1) * 128]
                    wu_t = wub[s % 2][:, fi * 512 + fo_l * 128:
                                      fi * 512 + (fo_l + 1) * 128]
                    for tt in range(NT):
                        rhs = xq[:, fi * TL + tt * 512:fi * TL + (tt + 1) * 512]
                        t.matmul(ctx.ps[tt][:, :], wg_t, rhs,
                                 start=(fi == 0), stop=(fi == 7))
                        ins = t.matmul(ctx.ps[4 + tt][:, :], wu_t, rhs,
                                       start=(fi == 0), stop=(fi == 7))
                cnt = ctx.inc("pe", ins)
                for bk in range(8):
                    ctx.set_bank_ready(bk, "pe", cnt)

            @blk.scalar
            def _(s_, fo=fo, hq_prev=hq_ve):
                if hq_prev is not None:
                    ctx.wait(s_, "ve", hq_prev)
                ins = None
                for tt in range(NT):
                    ctx.wait_bank_ready(s_, tt)
                    ins = s_.activation(SIL[:, tt * 512:(tt + 1) * 512],
                                        ctx.ps[tt][:, :], AF.Silu)
                cnt = ctx.inc("se", ins)
                for tt in range(NT):
                    ctx.release_bank(tt, "se", cnt)
            se_sil = ctx.val("se")

            @blk.vector
            def _(v, fo=fo, se_sil=se_sil):
                ctx.wait(v, "se", se_sil)
                ins = None
                for tt in range(NT):
                    ctx.wait_bank_ready(v, 4 + tt)
                    xsl = slice(tt * 512, (tt + 1) * 512)
                    ins = v.tensor_tensor(XF[:, xsl], SIL[:, xsl],
                                          ctx.ps[4 + tt][:, :], ALU.mult)
                cnt = ctx.inc("ve", ins)
                for tt in range(NT):
                    ctx.release_bank(4 + tt, "ve", cnt)
            hq_ve = emit_fp4_bfly(ctx, XF[:, :], SG[:, :], MM[:, :], TB[:, :],
                                  TI[:, :], QB[:, :], gsl(fo, 0, TL),
                                  ("ve", ctx.val("ve")))
        slab_pe[s % 2] = ("pe", ctx.val("pe"))
    hq_done = ("ve", hq_ve)

    if DEBUG_TAPS:
        nc2 = ctx.nc
        dbg_xq = nc2.dram_tensor("dbg_xq", [D, TL], F32,
                                 kind="ExternalOutput").ap()
        dbg_wg = nc2.dram_tensor("dbg_wg", [D, DHID], F32,
                                 kind="ExternalOutput").ap()
        dbg_wd = nc2.dram_tensor("dbg_wd", [DHID, D], F32,
                                 kind="ExternalOutput").ap()
        dbg_h = nc2.dram_tensor("dbg_h", [DHID, TL], F32,
                                kind="ExternalOutput").ap()

        @blk.gpsimd
        def _(g):
            ctx.wait(g, *hq_done)
            for c in range(NF):
                ins = g.dma_start(out=dbg_xq[c * 128:(c + 1) * 128, :],
                                  in_=xq[:, c * TL:(c + 1) * TL])
                ctx.inc("st", ins, 16)
            ins = g.dma_start(out=dbg_wg[:, :], in_=wg_q[:, :])
            ctx.inc("st", ins, 16)
            ins = g.dma_start(out=dbg_wd[:, :], in_=wd_q[:, :])
            ctx.inc("st", ins, 16)
            for c in range(NHC):
                ins = g.dma_start(out=dbg_h[c * 128:(c + 1) * 128, :],
                                  in_=gsl(c, 0, TL))
                ctx.inc("st", ins, 16)

    wb = [ctx.at("l2f_wb0", [128, 5632], BF16, A_WB0),
          ctx.at("l2f_wb1", [128, 5632], BF16, A_WB1)]
    xcb = ctx.at("l2f_xcb", [128, 4096], F32, TA)

    def hqsrc(fi, tt):
        return gsl(fi, tt * 512, (tt + 1) * 512)

    xc_ready = {}

    @blk.sync
    def _(sy):
        ctx.wait(sy, *hq_done)
        for fo in range(2):
            ins = sy.dma_start(out=xcb[:, fo * TL:(fo + 1) * TL],
                               in_=x_dram[fo * 128:(fo + 1) * 128, :])
            xc_ready[fo] = ("xld", ctx.inc("xld", ins, 16))

    def cbD(fo, banks):
        half = fo % 2

        @blk.vector
        def _(v, fo=fo, banks=banks, half=half):
            ctx.wait(v, *xc_ready[fo])
            ins = None
            for tt in range(NT):
                ctx.wait_bank_ready(v, banks[tt])
                sl = slice(half * TL + tt * 512, half * TL + (tt + 1) * 512)
                ins = v.tensor_tensor(xcb[:, sl], xcb[:, sl],
                                      ctx.ps[banks[tt]][:, :], ALU.add)
            cnt = ctx.inc("ve", ins)
            for tt in range(NT):
                ctx.release_bank(banks[tt], "ve", cnt)

        @blk.sync
        def _(sy, fo=fo, half=half):
            ctx.wait(sy, "ve")
            ins = sy.dma_start(out=y_dram[fo * 128:(fo + 1) * 128, :],
                               in_=xcb[:, half * TL:(half + 1) * TL])
            cnt = ctx.inc("st", ins, 16)
            if fo + 2 < 8:
                ctx.wait(sy, "st", cnt)
                ins2 = sy.dma_start(out=xcb[:, half * TL:(half + 1) * TL],
                                    in_=x_dram[(fo + 2) * 128:(fo + 3) * 128, :])
                xc_ready[fo + 2] = ("xld", ctx.inc("xld", ins2, 16))

    emit_bigproj(ctx, hqsrc, NHC, wd_q, 8, TL, cbD, wb, 2, src_ready=hq_done)

    @blk.sync
    def _(sy):
        ctx.wait(sy, "st")



SUB_WEIGHTS = {
    "L3_attn_a": weights_l3_attn_a,
    "L2_attn_b": weights_l2_attn_b,
    "L0_attn_b": weights_l0_attn_b,
    "L3_attn_b": weights_l3_attn_b,
    "L0_attn_a": weights_l0_attn_a,
    "L2_attn_a": weights_l2_attn_a,
    "L1_attn_a": weights_l1_attn_a,
    "L2_ffn": weights_l2_ffn,
    "L1_attn_b": weights_l1_attn_b,
    "L0_ffn": weights_l0_ffn,
    "L1_ffn": weights_l1_ffn,
    "L3_ffn": weights_l3_ffn,
}
SUB_EMIT = {
    "L3_attn_a": emit_l3_attn_a,
    "L2_attn_b": emit_l2_attn_b,
    "L0_attn_b": emit_l0_attn_b,
    "L3_attn_b": emit_l3_attn_b,
    "L0_attn_a": emit_l0_attn_a,
    "L2_attn_a": emit_l2_attn_a,
    "L1_attn_a": emit_l1_attn_a,
    "L2_ffn": emit_l2_ffn,
    "L1_attn_b": emit_l1_attn_b,
    "L0_ffn": emit_l0_ffn,
    "L1_ffn": emit_l1_ffn,
    "L3_ffn": emit_l3_ffn,
}
SUB_SELFSTORE = {"L1_ffn", "L3_ffn", "L1_attn_a", "L2_ffn", "L1_attn_b"}
SUB_XFULL = {"L1_attn_b"}
SUB_NEEDS_PEER = {"L0_attn_a", "L0_attn_b", "L1_attn_b", "L2_attn_a", "L2_attn_b",
                  "L3_attn_a", "L3_attn_b", "L1_attn_a"}

SEQUENCE = [
    ("L0_attn_a", "L0_attn_a"), ("L0_ffn", "L0_ffn"), ("L0_attn_b", "L0_attn_b"),
    ("L0_ffn", "L0_ffn"),
    ("L1_attn_a", "L1_attn_a"), ("L1_ffn", "L1_ffn"), ("L1_attn_b", "L1_attn_b"),
    ("L1_ffn", "L1_ffn"),
    ("L2_attn_a", "L2_attn_a"), ("L2_ffn", "L2_ffn"), ("L2_attn_b", "L2_attn_b"),
    ("L2_ffn", "L2_ffn"),
    ("L3_attn_a", "L3_attn_a"), ("L3_ffn", "L3_ffn"), ("L3_attn_b", "L3_attn_b"),
    ("L3_ffn", "L3_ffn"),
]


def build_sublayer_program(kind, wnames, core=0):
    nc = bass.Bass("TRN2", target_bir_lowering=False)
    stack = ExitStack()
    dram = {}
    xp_in = None
    if kind in SUB_XFULL:
        x_in = nc.dram_tensor("xfull_in", [D, T], F32,
                              kind="ExternalInput").ap()
        y_out = nc.dram_tensor("y_out", [D, T], F32, kind="ExternalOutput").ap()
    else:
        x_in = nc.dram_tensor("x_in", [D, TL], F32, kind="ExternalInput").ap()
        if kind in SUB_NEEDS_PEER:
            xp_in = nc.dram_tensor("xp_in", [D, TL], F32,
                                   kind="ExternalInput").ap()
        y_out = nc.dram_tensor("y_out", [D, TL], F32,
                               kind="ExternalOutput").ap()
    for name, shape in wnames.items():
        dram[name] = nc.dram_tensor(name, list(shape), F32, kind="ExternalInput").ap()
    cin = {}
    for name, shape, dt in CONST_SPECS:
        cin[name] = nc.dram_tensor("c_" + name, shape, F32, kind="ExternalInput").ap()

    with stack:
        block = stack.enter_context(nc.Block())
        ctx = Ctx(nc, block, stack)
        csb = {}
        off = OFF_CONST
        for name, shape, dt in CONST_SPECS:
            csb[name] = ctx.at("s_" + name, shape, dt, off)
            off += max(64, _pbytes(shape, dt))
        assert off <= OFF_MISC

        x_sb = ctx.at("x_sb", [128, NF * TL], F32, OFF_X)
        xln = ctx.at("xln", [128, NF * TL], BF16, OFF_XLN)
        sq = ctx.at("sq", [128, 2 * TL], BF16, LNS_SQ)
        rows = ctx.at("rows", [1, 2 * TL + 512], F32, LNS_ROWS)
        abrow = ctx.at("abrow", [1, 2 * TL], BF16, LNS_AB)

        @block.gpsimd
        def _(g):
            for name, shape, dt in CONST_SPECS:
                ins = g.dma_start(out=csb[name][:, :], in_=cin[name][:, :])
                ctx.inc("ld", ins, 16)

        if kind not in SUB_NEEDS_PEER and kind not in SUB_XFULL:
            emit_load_x(ctx, x_sb, x_in)
        ld0 = ctx.val("ld")

        for engname in ("tensor", "vector", "scalar"):
            @getattr(block, engname)
            def _(e):
                ctx.wait(e, "ld", ld0)

        st = {"x": x_sb, "xln": xln, "sq": sq, "rows": rows, "abrow": abrow,
              "dram": dram, "x_dram": x_in, "xp_dram": xp_in, "y_dram": y_out,
              "consts": {name: csb[name] for name, _s, _d in CONST_SPECS}}
        SUB_EMIT[kind](ctx, st, core)
        if kind not in SUB_SELFSTORE:
            emit_store_x(ctx, x_sb, y_out)
    return nc


SUB_PERCORE = {
    "L0_attn_a": lambda core: {"l0a_band": causal_band(core)},
    "L2_attn_a": lambda core: {"l2a_band": causal_band(core, anti=True,
                                                      bias=-1e9)},
    "L1_attn_a": l1a_percore,
}


_PROGS = {}


def run_sublayer(kind, x, inputs):
    wfn = SUB_WEIGHTS[kind]
    w = wfn(inputs)
    wnames = dict({k: v.shape for k, v in w.items()})
    pc0 = SUB_PERCORE.get(kind, lambda c: {})(0)
    wnames.update({k: v.shape for k, v in pc0.items()})
    if kind not in _PROGS:
        _PROGS[kind] = build_sublayer_program(kind, wnames)
    nc = _PROGS[kind]
    xf = np.asarray(x, dtype=np.float32)
    shards = shard_x(xf)
    in_maps = []
    for c in range(8):
        if kind in SUB_XFULL:
            m = {"xfull_in": np.ascontiguousarray(xf[c // 2].T)}
        else:
            m = {"x_in": shards[c]}
            if kind in SUB_NEEDS_PEER:
                m["xp_in"] = shards[c ^ 1]
        m.update(w)
        m.update(SUB_PERCORE.get(kind, lambda cc: {})(c))
        for name, arr in host_consts(c).items():
            m["c_" + name] = arr
        in_maps.append(m)
    res = run_bass_kernel_spmd(nc, in_maps, core_ids=list(range(8)))
    if kind in SUB_XFULL:
        return np.stack([np.asarray(res.results[2 * b]["y_out"]).T
                         for b in range(B)])
    return unshard_x([res.results[c]["y_out"] for c in range(8)])


def kernel(**inputs):
    x = np.asarray(inputs["L0_attn_a_x"], dtype=np.float32)
    for kind, _name in SEQUENCE:
        x = run_sublayer(kind, x, inputs)
    return np.ascontiguousarray(np.asarray(x, dtype=np.float32))
